# version 17
# baseline (speedup 1.0000x reference)
import numpy as np
import ml_dtypes
from contextlib import ExitStack

import concourse.bass as bass
import concourse.mybir as mybir
from concourse.bass_utils import run_bass_kernel_spmd

F32 = mybir.dt.float32
BF16 = mybir.dt.bfloat16
AF = mybir.ActivationFunctionType
ALU = mybir.AluOpType
AX = mybir.AxisListType

E = 1024
FF = 2048
NEXP = 8
ET = E // 128
FT = FF // 128
NCORE = 8

NEG_BIG = -1.0e30


class Sem:
    def __init__(self, stack, nc, name):
        self.h = stack.enter_context(nc.semaphore(name))
        self.n = 0

    def inc(self, instr, k=1):
        instr.then_inc(self.h, k)
        self.n += k
        return self.n


class Builder:
    def __init__(self, T=4096, taps=()):
        self.T = T
        self.NT = T // 128
        self.NIC = T // 512
        self.NIC2 = T // 256
        self.PTOK = min(1024, T)
        self.NP = T // self.PTOK
        self.PS = self.PTOK // 128
        self.taps = set(taps)
        nc = bass.Bass("TRN2", target_bir_lowering=False,
                       detect_race_conditions=False)
        self.nc = nc
        self.stack = ExitStack()
        self._waited = {}

        dt_in = lambda nm, shp: nc.dram_tensor(nm, shp, F32, kind="ExternalInput").ap()
        T_ = self.T
        self.x = dt_in("x", [T_, E])
        self.wqT = dt_in("wqT", [E, E])
        self.wkT = dt_in("wkT", [E, E])
        self.wvT = dt_in("wvT", [E, E])
        self.bq_pp = dt_in("bq_pp", [128, ET])
        self.bk_pp = dt_in("bk_pp", [128, ET])
        self.bv_r1 = dt_in("bv_r1", [1, E])
        self.rwT_pp = dt_in("rwT_pp", [128, ET * NEXP])
        self.gate_b_pp = dt_in("gate_b_pp", [NEXP, 128, FT])
        self.up_b_pp = dt_in("up_b_pp", [NEXP, 128, FT])
        self.down_b_n8 = dt_in("down_b_n8", [NEXP, E])
        self.gw = dt_in("gw", [NEXP, 128, FT * E])
        self.uw = dt_in("uw", [NEXP, 128, FT * E])
        self.dw = dt_in("dw", [NEXP, 128, FT * E])
        self.out = nc.dram_tensor("out", [T_, E], F32, kind="ExternalOutput").ap()
        self.qT_spill = nc.dram_tensor("qT_spill", [128, ET * T_], BF16, kind="Internal").ap()
        self.v_spill = nc.dram_tensor("v_spill", [128, self.NT * E], BF16, kind="Internal").ap()
        self.pt_spill = nc.dram_tensor("pt_spill", [128, self.NT * T_], BF16, kind="Internal").ap()
        self.acc_spill = nc.dram_tensor("acc_spill", [128, self.NT * E], F32, kind="Internal").ap()
        self.probsT_hbm = nc.dram_tensor("probsT_hbm", [NEXP, self.NT * 128], BF16, kind="Internal").ap()
        self.gwc = nc.dram_tensor("gwc", [NEXP, 128, FT * E], BF16, kind="Internal").ap()
        self.uwc = nc.dram_tensor("uwc", [NEXP, 128, FT * E], BF16, kind="Internal").ap()
        self.dwc = nc.dram_tensor("dwc", [NEXP, 128, FT * E], BF16, kind="Internal").ap()

        ident_np = np.eye(128, dtype=ml_dtypes.bfloat16)
        masks_np = np.zeros((128, 4 * 512), dtype=np.float32)
        for m in range(4):
            o = m * 128
            for p in range(128):
                thr = min(o + p, 512)
                masks_np[p, m * 512: m * 512 + thr] = NEG_BIG
        ones_np = np.ones((1, 128), dtype=ml_dtypes.bfloat16)
        consts_np = np.zeros((128, 4), dtype=np.float32)
        consts_np[:, 0] = 1e-5
        consts_np[:, 1] = -0.5
        consts_np[:, 2] = 1.0 / 32.0
        consts_np[:, 3] = -1.0
        self.ident_dram = nc.inline_tensor(ident_np, name="ident_dram")
        self.masks_dram = nc.inline_tensor(masks_np.astype(ml_dtypes.bfloat16), name="masks_dram")
        self.ones_dram = nc.inline_tensor(ones_np, name="ones_dram")
        self.consts_dram = nc.inline_tensor(consts_np, name="consts_dram")

        S = lambda nm: Sem(self.stack, nc, nm)
        self.t_c = S("t_c")
        self.v_c = S("v_c")
        self.s_c = S("s_c")
        self.d_const = S("d_const")
        self.d_x = S("d_x")
        self.d_qout = S("d_qout")
        self.d_vout = S("d_vout")
        self.d_ptout = S("d_ptout")
        self.d_qin = S("d_qin")
        self.d_vin = S("d_vin")
        self.d_ptin = S("d_ptin")
        self.d_accout = S("d_accout")
        self.d_accin = S("d_accin")
        self.d_out = S("d_out")
        self.g_w = S("g_w")
        self.g_moe = S("g_moe")
        self.d_wc = S("d_wc")
        self.d_wl = S("d_wl")

        sb = self.sb = lambda nm, shp, dt=F32: self.stack.enter_context(
            nc.sbuf_tensor(nm, shp, dt))
        self.ident = sb("ident", [128, 128], BF16)
        self.ones_bf = sb("ones_bf", [1, 128], BF16)
        self.consts = sb("consts", [128, 4])
        self.bq_sb = sb("bq_sb", [128, ET])
        self.bks_sb = sb("bks_sb", [128, ET])
        self.bv_bf = sb("bv_bf", [1, E], BF16)
        self.rwT_bf = sb("rwT_bf", [128, ET * NEXP], BF16)
        self.gate_b_sb = sb("gate_b_sb", [128, NEXP * FT])
        self.up_b_sb = sb("up_b_sb", [128, NEXP * FT])
        self.xnT = sb("xnT", [128, ET * T_], BF16)
        self.probs = sb("probs", [128, self.NT * NEXP])
        self.corr = sb("corr", [128, self.NT * self.NIC])
        self.dbg = {}

    def wait(self, eng_name, sem, val):
        if val <= 0:
            return
        key = (eng_name, id(sem))
        if self._waited.get(key, -1) >= val:
            return
        self._waited[key] = val
        getattr(self.nc, eng_name).wait_ge(sem.h, val)

    def vfence(self, instr):
        tok = self.v_c.inc(instr)
        self.nc.vector.wait_ge(self.v_c.h, tok)
        return tok

    def sfence(self, instr):
        tok = self.s_c.inc(instr)
        self.nc.scalar.wait_ge(self.s_c.h, tok)
        return tok

    def barrier(self):
        self.nc.all_engine_barrier()
        self._waited.clear()

    def tap(self, name, ap, dt=F32, shape=None):
        if name not in self.taps:
            return
        shp = shape or list(ap.shape)
        dram = self.nc.dram_tensor("tap_" + name, shp, dt, kind="ExternalOutput").ap()
        s = Sem(self.stack, self.nc, "tapsem_" + name)
        self.nc.sync.dma_start(out=dram, in_=ap).then_inc(s.h, 16)
        self.nc.sync.wait_ge(s.h, 16)
        self.dbg[name] = dram

    def emit_setup(self):
        nc = self.nc
        sy, gp = nc.sync, nc.gpsimd
        dc, gw_ = self.d_const, self.g_w
        dc.inc(sy.dma_start(out=self.ident[:], in_=self.ident_dram.ap()), 16)
        dc.inc(sy.dma_start(out=self.ones_bf[:], in_=self.ones_dram.ap()), 16)
        dc.inc(sy.dma_start(out=self.consts[:], in_=self.consts_dram.ap()), 16)
        dc.inc(sy.dma_start(out=self.bq_sb[:], in_=self.bq_pp), 16)
        dc.inc(sy.dma_start(out=self.bks_sb[:], in_=self.bk_pp), 16)
        dc.inc(sy.dma_start(out=self.gate_b_sb.ap().rearrange("p (e f) -> p e f", f=FT),
                            in_=self.gate_b_pp.rearrange("e p f -> p e f")), 16)
        dc.inc(sy.dma_start(out=self.up_b_sb.ap().rearrange("p (e f) -> p e f", f=FT),
                            in_=self.up_b_pp.rearrange("e p f -> p e f")), 16)
        self.const_tok = dc.n
        self.tok_bv = gw_.inc(gp.dma_start(out=self.bv_bf[:], in_=self.bv_r1), 16)
        self.tok_rw = gw_.inc(gp.dma_start(out=self.rwT_bf[:], in_=self.rwT_pp), 16)

    def emit_fused01(self):
        nc, T = self.nc, self.T
        st = ExitStack()
        sb = lambda nm, shp, dt=F32: st.enter_context(nc.sbuf_tensor(nm, shp, dt))
        x_sb = [sb(f"ln_x{b}", [128, E]) for b in range(4)]
        tmp = sb("ln_tmp", [128, E])
        xnb = [sb(f"ln_xn{b}", [128, E], BF16) for b in range(4)]
        stat2 = sb("ln_stat", [128, 128])
        wv_sb = sb("wv_sb", [128, ET * E], BF16)
        wk_sb = sb("wk_sb", [128, ET * E], BF16)
        wqb = [sb(f"wqb{b}", [128, ET * 128], BF16) for b in range(2)]
        vst = [sb(f"vst{b}", [128, E], BF16) for b in range(2)]
        qst = [sb(f"qst{b}", [128, 512], BF16) for b in range(2)]
        pst = [st.enter_context(nc.psum_tensor(f"ln_ps{b}", [128, 1024], BF16))
               for b in range(4)]
        ps = [st.enter_context(nc.psum_tensor(f"p1_ps{b}", [128, 512], F32))
              for b in range(4)]
        if "lnstat" in self.taps:
            self.lnstat = self.sb("lnstat", [128, self.NT * 8])
        if "lnx" in self.taps:
            self.lnx = self.sb("lnx", [128, self.NT * E])
        wtok = {}
        for nm, dst, srcw in (("v", wv_sb, self.wvT), ("k", wk_sb, self.wkT)):
            wtok[nm] = self.g_w.inc(nc.gpsimd.dma_start(
                out=dst.ap().rearrange("p (et f) -> p et f", f=E),
                in_=srcw.rearrange("(et p) f -> p et f", p=128)), 16)
        wq_r = self.wqT.rearrange("(et p) f -> p et f", p=128)
        x_r = self.x.rearrange("(tt p) e -> tt p e", p=128)
        B0 = 4 if self.NT % 4 == 0 else 1
        NB = self.NT // B0
        x_free = [0] * B0
        ps_tok = [0] * 4
        vst_free = [0, 0]
        self.wait("scalar", self.d_const, self.const_tok)
        self.wait("tensor", self.d_const, self.const_tok)
        self.wait("tensor", self.g_w, wtok["k"])
        ln_rd = [0, 0]
        norm_tok = [[0] * B0 for _ in range(2)]
        xn_guard = [0]

        def emit_stats(bb):
            par = bb % 2
            stat = stat2[:, par * 64: par * 64 + 64]
            x_dma = [0] * B0
            for i in range(B0):
                tt = bb * B0 + i
                if bb >= 1:
                    self.wait("sync", self.v_c, x_free[i])
                x_dma[i] = self.d_x.inc(nc.sync.dma_start(out=x_sb[i][:], in_=x_r[tt]), 16)
            self.wait("vector", self.d_x, x_dma[B0 - 1])
            self.wait("vector", self.s_c, ln_rd[par])
            last = None
            for i in range(B0):
                last = nc.vector.reduce_sum(stat[:, i:i + 1], x_sb[i][:], axis=AX.X)
            self.vfence(last)
            self.vfence(nc.vector.tensor_scalar_mul(out=stat[:, 8:8 + B0],
                                                    in0=stat[:, 0:B0], scalar1=1.0 / E))
            vtok = self.vfence(nc.vector.tensor_mul(stat[:, 16:16 + B0],
                                                    stat[:, 8:8 + B0], stat[:, 8:8 + B0]))
            self.wait("scalar", self.d_x, x_dma[B0 - 1])
            self.wait("scalar", self.v_c, vtok)
            last = None
            for i in range(B0):
                last = nc.scalar.activation(out=tmp[:], in_=x_sb[i][:],
                                            func=AF.Square, accum_out=stat[:, 24 + i:25 + i])
            stok = self.s_c.inc(last)
            self.wait("vector", self.s_c, stok)
            var_tok = self.vfence(nc.vector.scalar_tensor_tensor(
                out=stat[:, 32:32 + B0], in0=stat[:, 24:24 + B0], scalar=1.0 / E,
                in1=stat[:, 16:16 + B0], op0=ALU.mult, op1=ALU.subtract))
            self.wait("scalar", self.v_c, var_tok)
            self.sfence(nc.scalar.activation(out=stat[:, 40:40 + B0], in_=stat[:, 32:32 + B0],
                                             func=AF.Ln, bias=self.consts[:, 0:1], scale=1.0))
            stok = self.s_c.inc(nc.scalar.activation(out=stat[:, 48:48 + B0],
                                                     in_=stat[:, 40:40 + B0],
                                                     func=AF.Exp, scale=self.consts[:, 1:2]))
            ln_rd[par] = stok
            self.wait("vector", self.s_c, stok)
            self.vfence(nc.vector.tensor_scalar_mul(out=stat[:, 56:56 + B0],
                                                    in0=stat[:, 8:8 + B0], scalar1=-1.0))
            self.wait("vector", self.t_c, xn_guard[0])
            for i in range(B0):
                tt = bb * B0 + i
                nc.vector.tensor_single_scalar(out=tmp[:], in_=x_sb[i][:],
                                               scalar=stat[:, 56 + i:57 + i], op=ALU.add)
                norm_tok[par][i] = self.v_c.inc(nc.vector.tensor_single_scalar(
                    out=xnb[i][:], in_=tmp[:], scalar=stat[:, 48 + i:49 + i], op=ALU.mult))
                x_free[i] = norm_tok[par][i]
                if "lnstat" in self.taps:
                    nc.vector.tensor_copy(out=self.lnstat[:, tt * 8 + 0:tt * 8 + 1],
                                          in_=stat[:, 8 + i:9 + i])
                    nc.vector.tensor_copy(out=self.lnstat[:, tt * 8 + 1:tt * 8 + 2],
                                          in_=stat[:, 32 + i:33 + i])

        mmpool = [(None, 0)] * 4
        pool_i = [0]

        def pool_wait(j):
            sem, val = mmpool[j]
            if sem is not None:
                self.wait("tensor", sem, val)

        emit_stats(0)
        for bb in range(NB):
            par = bb % 2
            for i in range(B0):
                tt = bb * B0 + i
                self.wait("tensor", self.v_c, norm_tok[par][i])
                for et in range(ET):
                    pb = et % 4
                    self.wait("tensor", self.v_c if pb % 2 == 0 else self.s_c, ps_tok[pb])
                    tp_tok = self.t_c.inc(nc.tensor.transpose(
                        pst[pb][:, 0:128], xnb[i][:, et * 128:(et + 1) * 128], self.ident[:]))
                    dst = self.xnT[:, et * T + tt * 128: et * T + (tt + 1) * 128]
                    if pb % 2 == 0:
                        self.wait("vector", self.t_c, tp_tok)
                        ps_tok[pb] = self.v_c.inc(nc.vector.tensor_copy(out=dst, in_=pst[pb][:, 0:128]))
                    else:
                        self.wait("scalar", self.t_c, tp_tok)
                        ps_tok[pb] = self.s_c.inc(nc.scalar.copy(out=dst, in_=pst[pb][:, 0:128]))
            xn_guard[0] = self.t_c.n
            xnT_v = self.v_c.n
            xnT_s = self.s_c.n
            if bb + 1 < NB:
                emit_stats(bb + 1)
            self.wait("tensor", self.s_c, xnT_s)
            for i in range(B0):
                tt = bb * B0 + i
                sbuf = vst[tt % 2]
                for fh in range(2):
                    j = pool_i[0] % 4
                    pool_i[0] += 1
                    pool_wait(j)
                    pool_wait(pool_i[0] % 4)
                    for et in range(ET):
                        nc.tensor.matmul(ps[j][:],
                                         self.xnT[:, et * T + tt * 128: et * T + (tt + 1) * 128],
                                         wv_sb[:, et * E + fh * 512: et * E + (fh + 1) * 512],
                                         start=(et == 0), stop=False)
                    ttok = self.t_c.inc(nc.tensor.matmul(
                        ps[j][:], self.ones_bf[:], self.bv_bf[:, fh * 512:(fh + 1) * 512],
                        start=False, stop=True))
                    self.wait("vector", self.t_c, ttok)
                    if tt >= 2 and fh == 0:
                        self.wait("vector", self.d_vout, vst_free[tt % 2])
                    mmpool[j] = (self.v_c, self.v_c.inc(nc.vector.tensor_copy(
                        out=sbuf[:, fh * 512:(fh + 1) * 512], in_=ps[j][:])))
                self.wait("sync", self.v_c, self.v_c.n)
                vst_free[tt % 2] = self.d_vout.inc(nc.sync.dma_start(
                    out=self.v_spill[:, tt * E:(tt + 1) * E], in_=sbuf[:]), 16)
            tc = bb
            for ft in range(ET):
                j = pool_i[0] % 4
                pool_i[0] += 1
                pool_wait(j)
                pool_wait(pool_i[0] % 4)
                for et in range(ET):
                    mm = nc.tensor.matmul(
                        ps[j][:],
                        wk_sb[:, et * E + ft * 128: et * E + (ft + 1) * 128],
                        self.xnT[:, et * T + tc * 512: et * T + (tc + 1) * 512],
                        start=(et == 0), stop=(et == ET - 1))
                ttok = self.t_c.inc(mm)
                self.wait("vector", self.t_c, ttok)
                mmpool[j] = (self.v_c, self.v_c.inc(nc.vector.tensor_single_scalar(
                    out=self.kT[:, ft * T + tc * 512: ft * T + (tc + 1) * 512],
                    in_=ps[j][:], scalar=self.bks_sb[:, ft:ft + 1], op=ALU.add)))
        self.tap("xnT", self.xnT[:], BF16)
        if "lnstat" in self.taps:
            self.tap("lnstat", self.lnstat[:])
        if "lnx" in self.taps:
            self.tap("lnx", self.lnx[:])
        self.wait("vector", self.d_const, self.const_tok)
        qst_free = [0, 0]
        wq_tok = [0, 0]
        wq_free = [0, 0]
        for ft in range(2):
            wq_tok[ft] = self.g_w.inc(nc.gpsimd.dma_start(
                out=wqb[ft].ap().rearrange("p (et f) -> p et f", f=128),
                in_=wq_r[:, :, ft * 128:(ft + 1) * 128]), 16)
        nq = 0
        for ft in range(ET):
            gb = ft % 2
            self.wait("tensor", self.g_w, wq_tok[gb])
            for tc in range(T // 512):
                pb = pool_i[0] % 4
                pool_i[0] += 1
                pool_wait(pb)
                pool_wait(pool_i[0] % 4)
                for et in range(ET):
                    mm = nc.tensor.matmul(ps[pb][:],
                                          wqb[gb][:, et * 128:(et + 1) * 128],
                                          self.xnT[:, et * T + tc * 512: et * T + (tc + 1) * 512],
                                          start=(et == 0), stop=(et == ET - 1))
                ttok = self.t_c.inc(mm)
                sbuf = qst[nq % 2]
                self.wait("vector", self.t_c, ttok)
                self.wait("vector", self.d_qout, qst_free[nq % 2])
                mmpool[pb] = (self.v_c, self.v_c.inc(nc.vector.tensor_single_scalar(
                    out=sbuf[:], in_=ps[pb][:], scalar=self.bq_sb[:, ft:ft + 1],
                    op=ALU.add)))
                self.v_c.inc(nc.vector.tensor_scalar_mul(out=sbuf[:], in0=sbuf[:],
                                                         scalar1=1.0 / 32.0))
                self.wait("sync", self.v_c, self.v_c.n)
                qst_free[nq % 2] = self.d_qout.inc(nc.sync.dma_start(
                    out=self.qT_spill[:, ft * T + tc * 512: ft * T + (tc + 1) * 512],
                    in_=sbuf[:]), 16)
                nq += 1
            wq_free[gb] = self.t_c.n
            if ft + 2 < ET:
                self.wait("gpsimd", self.t_c, wq_free[gb])
                wq_tok[gb] = self.g_w.inc(nc.gpsimd.dma_start(
                    out=wqb[gb].ap().rearrange("p (et f) -> p et f", f=128),
                    in_=wq_r[:, :, (ft + 2) * 128:(ft + 3) * 128]), 16)
        self.barrier()
        st.close()

    def emit_A1(self):
        nc, T = self.nc, self.T
        st = ExitStack()
        sb = lambda nm, shp, dt=F32: st.enter_context(nc.sbuf_tensor(nm, shp, dt))
        qc = [sb(f"a1_qc{b}", [128, ET * 512], BF16) for b in range(2)]
        stg = [sb(f"a1_st{b}", [128, 512], BF16) for b in range(2)]
        pts = [sb(f"a1_pt{b}", [128, 512], BF16) for b in range(4)]
        self.masks = sb("masks", [128, 4 * 512], BF16)
        self.tmaxn = sb("tmaxn", [128, self.NT * self.NIC])
        self.sloc = sb("sloc", [128, self.NT * self.NIC])
        mask_tok = self.d_const.inc(nc.sync.dma_start(out=self.masks[:],
                                                      in_=self.masks_dram.ap()), 16)
        NPS = 4
        ps = [st.enter_context(nc.psum_tensor(f"a1_ps{b}", [128, 512], F32))
              for b in range(NPS)]
        u = sb("rt_u", [128, NEXP])
        S_ = sb("rt_S", [128, 1])
        negm = sb("rt_negm", [128, 1])
        ge1 = sb("rt_ge1", [128, NEXP])
        pm = sb("rt_pm", [128, NEXP])
        m2 = sb("rt_m2", [128, 1])
        keep = sb("rt_keep", [128, NEXP])
        pk = sb("rt_pk", [128, NEXP])
        s2 = sb("rt_s2", [128, 1])
        den = sb("rt_den", [128, 1])
        rD = sb("rt_rD", [128, 1])
        probs_bf = sb("rt_pbf", [128, NEXP], BF16)
        rps = [st.enter_context(nc.psum_tensor(f"rt_ps{b}", [128, NEXP], F32))
               for b in range(2)]
        rps_t = st.enter_context(nc.psum_tensor("rt_pst", [NEXP, 128], BF16))
        rps_t2 = st.enter_context(nc.psum_tensor("rt_pst2", [NEXP, 128], BF16))
        rt_state = {"ps_free": [0, 0], "u_free": 0}

        def emit_router_tile(tt):
            pb = tt % 2
            self.wait("tensor", self.s_c, rt_state["ps_free"][pb])
            for et in range(ET):
                mm = nc.tensor.matmul(rps[pb][:],
                                      self.xnT[:, et * T + tt * 128: et * T + (tt + 1) * 128],
                                      self.rwT_bf[:, et * NEXP:(et + 1) * NEXP],
                                      start=(et == 0), stop=(et == ET - 1))
            ttok = self.t_c.inc(mm)
            self.wait("vector", self.t_c, ttok)
            vtok = self.v_c.inc(nc.vector.reduce_max(negm[:], rps[pb][:], axis=AX.X,
                                                     negate=True))
            self.wait("scalar", self.v_c, vtok)
            self.wait("scalar", self.v_c, rt_state["u_free"])
            rt_state["ps_free"][pb] = self.s_c.inc(nc.scalar.activation(
                out=u[:], in_=rps[pb][:], func=AF.Exp, bias=negm[:], scale=1.0,
                accum_out=S_[:]))
            self.wait("vector", self.s_c, rt_state["ps_free"][pb])
            self.vfence(nc.vector.tensor_single_scalar(out=ge1[:], in_=u[:], scalar=1.0,
                                                       op=ALU.is_ge))
            self.vfence(nc.vector.scalar_tensor_tensor(out=pm[:], in0=ge1[:], scalar=NEG_BIG,
                                                       in1=u[:], op0=ALU.mult, op1=ALU.add))
            self.vfence(nc.vector.reduce_max(m2[:], pm[:], axis=AX.X))
            self.vfence(nc.vector.tensor_single_scalar(out=keep[:], in_=u[:], scalar=m2[:],
                                                       op=ALU.is_ge))
            self.vfence(nc.vector.tensor_mul(pk[:], u[:], keep[:]))
            self.vfence(nc.vector.reduce_sum(s2[:], pk[:], axis=AX.X))
            self.vfence(nc.vector.scalar_tensor_tensor(out=den[:], in0=S_[:], scalar=1e-9,
                                                       in1=s2[:], op0=ALU.mult, op1=ALU.add))
            self.vfence(nc.vector.reciprocal(rD[:], den[:]))
            rt_state["u_free"] = self.v_c.inc(nc.vector.tensor_scalar_mul(
                out=self.probs[:, tt * NEXP:(tt + 1) * NEXP], in0=pk[:], scalar1=rD[:]))
        qsp = self.qT_spill.rearrange("p (et t) -> p et t", t=T)
        self.wait("vector", self.d_const, mask_tok)
        self.wait("scalar", self.d_const, self.const_tok)
        nc.vector.memset(self.tmaxn[:], 1.0e30)
        nc.vector.memset(self.sloc[:], 0.0)
        NIC = self.NIC
        ebuf = sb("a1_ebuf", [128, self.NT * NIC])
        zbuf = sb("a1_zbuf", [128, self.NT * NIC])
        minS = sb("a1_minS", [128, self.NT])
        Z = sb("a1_Z", [128, self.NT])
        rZ = sb("a1_rZ", [128, self.NT])
        epi_next = [0]

        def epi_batch(j1):
            j0 = epi_next[0]
            if j0 >= j1:
                return
            epi_next[0] = j1
            self.wait("vector", self.s_c, self.s_c.n)
            last = None
            for jt in range(j0, j1):
                last = nc.vector.tensor_reduce(minS[:, jt:jt + 1],
                                               self.tmaxn[:, jt * NIC:(jt + 1) * NIC],
                                               axis=AX.X, op=ALU.min)
            vtok = self.vfence(last)
            self.wait("scalar", self.v_c, vtok)
            for jt in range(j0, j1):
                last = nc.scalar.activation(out=ebuf[:, jt * NIC:(jt + 1) * NIC],
                                            in_=self.tmaxn[:, jt * NIC:(jt + 1) * NIC],
                                            func=AF.Exp, bias=minS[:, jt:jt + 1],
                                            scale=self.consts[:, 3:4])
            ep_s = self.s_c.inc(last)
            self.wait("vector", self.s_c, ep_s)
            self.vfence(nc.vector.tensor_mul(zbuf[:, j0 * NIC:j1 * NIC],
                                             ebuf[:, j0 * NIC:j1 * NIC],
                                             self.sloc[:, j0 * NIC:j1 * NIC]))
            last = None
            for jt in range(j0, j1):
                last = nc.vector.reduce_sum(Z[:, jt:jt + 1],
                                            zbuf[:, jt * NIC:(jt + 1) * NIC], axis=AX.X)
            self.vfence(last)
            self.vfence(nc.vector.reciprocal(rZ[:, j0:j1], Z[:, j0:j1]))
            last = None
            for jt in range(j0, j1):
                last = nc.vector.tensor_scalar_mul(
                    out=self.corr[:, jt * NIC:(jt + 1) * NIC],
                    in0=ebuf[:, jt * NIC:(jt + 1) * NIC],
                    scalar1=rZ[:, jt:jt + 1])
            self.v_c.inc(last)

        qc_tok = [0, 0]
        qc_free = [0, 0]
        ps_free = [(None, 0)] * NPS
        stg_free = [0, 0]
        pts_free = [0, 0, 0, 0]
        nt = 0
        rt_next = [0]
        self.wait("tensor", self.g_w, self.tok_rw)
        qc_tok[0] = self.d_qin.inc(nc.sync.dma_start(
            out=qc[0].ap().rearrange("p (et c) -> p et c", c=512),
            in_=qsp[:, :, 0:512]), 16)
        for ic in range(self.NIC):
            b = ic % 2
            if ic + 1 < self.NIC:
                nb_ = (ic + 1) % 2
                self.wait("sync", self.t_c, qc_free[nb_])
                qc_tok[nb_] = self.d_qin.inc(nc.sync.dma_start(
                    out=qc[nb_].ap().rearrange("p (et c) -> p et c", c=512),
                    in_=qsp[:, :, (ic + 1) * 512:(ic + 2) * 512]), 16)
            njt = min(4 * ic + 4, self.NT)
            self.wait("tensor", self.d_qin, qc_tok[b])
            for jt in range(njt):
                pb = nt % NPS
                fsem, fval = ps_free[pb]
                if fsem is not None:
                    self.wait("tensor", fsem, fval)
                for ft in range(ET):
                    mm = nc.tensor.matmul(
                        ps[pb][:],
                        self.kT[:, ft * T + jt * 128: ft * T + (jt + 1) * 128],
                        qc[b][:, ft * 512:(ft + 1) * 512],
                        start=(ft == 0), stop=(ft == ET - 1))
                ttok = self.t_c.inc(mm)
                sidx = jt * self.NIC + ic
                m = jt - 4 * ic
                pt = pts[nt % 4]
                if 0 <= m < 4:
                    sg = stg[nt % 2]
                    self.wait("vector", self.t_c, ttok)
                    self.wait("vector", self.s_c, stg_free[nt % 2])
                    sgi = nc.vector.tensor_tensor(
                        out=sg[:], in0=ps[pb][:],
                        in1=self.masks[:, m * 512:(m + 1) * 512], op=ALU.add)
                    ps_free[pb] = (self.v_c, self.vfence(sgi))
                    vtok = self.v_c.inc(nc.vector.reduce_max(
                        self.tmaxn[:, sidx:sidx + 1], sg[:], axis=AX.X, negate=True))
                    self.wait("scalar", self.v_c, vtok)
                    self.wait("scalar", self.d_ptout, pts_free[nt % 4])
                    stg_free[nt % 2] = self.s_c.inc(nc.scalar.activation(
                        out=pt[:], in_=sg[:], func=AF.Exp,
                        bias=self.tmaxn[:, sidx:sidx + 1], scale=1.0,
                        accum_out=self.sloc[:, sidx:sidx + 1]))
                else:
                    self.wait("vector", self.t_c, ttok)
                    vtok = self.v_c.inc(nc.vector.reduce_max(
                        self.tmaxn[:, sidx:sidx + 1], ps[pb][:], axis=AX.X,
                        negate=True))
                    self.wait("scalar", self.v_c, vtok)
                    self.wait("scalar", self.d_ptout, pts_free[nt % 4])
                    ps_free[pb] = (self.s_c, self.s_c.n + 1)
                    self.s_c.inc(nc.scalar.activation(
                        out=pt[:], in_=ps[pb][:], func=AF.Exp,
                        bias=self.tmaxn[:, sidx:sidx + 1], scale=1.0,
                        accum_out=self.sloc[:, sidx:sidx + 1]))
                self.wait("sync", self.s_c, self.s_c.n)
                pts_free[nt % 4] = self.d_ptout.inc(nc.sync.dma_start(
                    out=self.pt_spill[:, jt * T + ic * 512: jt * T + (ic + 1) * 512],
                    in_=pt[:]), 16)
                nt += 1
                if nt % 4 == 2 and rt_next[0] < self.NT:
                    emit_router_tile(rt_next[0])
                    rt_next[0] += 1
                if ic == self.NIC - 1 and (jt + 1) % 8 == 0:
                    epi_batch(jt + 1)
            qc_free[b] = self.t_c.n
        while rt_next[0] < self.NT:
            emit_router_tile(rt_next[0])
            rt_next[0] += 1
        self.tap("probs", self.probs[:])
        epi_batch(self.NT)
        probsT_sb = sb("probsT_sb", [NEXP, self.NT * 128], BF16)
        probs_bf_all = sb("probs_bf_all", [128, self.NT * NEXP], BF16)
        self.wait("vector", self.s_c, self.s_c.n)
        pba = self.v_c.inc(nc.vector.tensor_copy(out=probs_bf_all[:], in_=self.probs[:]))
        self.wait("tensor", self.v_c, pba)
        pstf = [0, 0]
        for tt in range(self.NT):
            self.wait("tensor", self.v_c, pstf[tt % 2])
            tpk = self.t_c.inc(nc.tensor.transpose(
                rps_t[:] if tt % 2 == 0 else rps_t2[:],
                probs_bf_all[:, tt * NEXP:(tt + 1) * NEXP], self.ident[:]))
            self.wait("vector", self.t_c, tpk)
            pstf[tt % 2] = self.v_c.inc(nc.vector.tensor_copy(
                out=probsT_sb[:, tt * 128:(tt + 1) * 128],
                in_=rps_t[:] if tt % 2 == 0 else rps_t2[:]))
        self.wait("sync", self.v_c, self.v_c.n)
        self.d_ptout.inc(nc.sync.dma_start(out=self.probsT_hbm, in_=probsT_sb[:]), 16)
        self.a1_final = (self.t_c.n, self.v_c.n, self.s_c.n)
        st.close()
        self.tap("corr", self.corr[:])

    def emit_A2(self):
        nc, T = self.nc, self.T
        st = ExitStack()
        sb = lambda nm, shp, dt=F32: st.enter_context(nc.sbuf_tensor(nm, shp, dt))
        NPTS = 6
        v_sb = sb("a2_v", [128, self.NT * E], BF16)
        ptin = [sb(f"a2_ptin{b}", [128, 4 * 256], BF16) for b in range(3)]
        pts = [sb(f"a2_pts{b}", [128, 256], BF16) for b in range(NPTS)]
        x_sb = [sb(f"a2_x{b}", [128, 2 * E]) for b in range(2)]
        acc = [sb(f"a2_acc{b}", [128, 2 * E]) for b in range(2)]
        ps = [[st.enter_context(nc.psum_tensor(f"a2_ps{par}_{k}", [128, 512], F32))
               for k in range(4)] for par in range(2)]
        t_f, v_f, s_f = self.a1_final
        self.wait("sync", self.t_c, t_f)
        VC = 8 * E
        nvc = (self.NT * E + VC - 1) // VC
        v_tok = []
        for c in range(nvc):
            lo = c * VC
            hi = min((c + 1) * VC, self.NT * E)
            v_tok.append(self.d_vin.inc(nc.sync.dma_start(
                out=v_sb[:, lo:hi], in_=self.v_spill[:, lo:hi]), 16))
        self.wait("sync", self.v_c, v_f)
        self.wait("sync", self.s_c, s_f)
        self.wait("vector", self.s_c, s_f)
        self.wait("tensor", self.s_c, s_f)
        ptsp = self.pt_spill.rearrange("p (jt t) -> p jt t", t=T)
        x_r = self.x.rearrange("(tt p) e -> p tt e", p=128)
        ptin_tok = [0, 0, 0]
        ptin_free = [0, 0, 0]
        pts_free = [0] * NPTS
        ps_free = [0, 0]
        x_free = [0, 0]
        acc_free = [0, 0]
        groups = []
        for ic2 in range(self.NIC2):
            njt = min(2 * ic2 + 2, self.NT)
            for jt0 in range(0, njt, 4):
                groups.append((ic2, jt0, min(4, njt - jt0)))
        n_groups = len(groups)
        issued = 0

        def issue_group():
            nonlocal issued
            if issued >= n_groups:
                return
            g_ic2, g_jt0, g_cnt = groups[issued]
            gb = issued % 3
            self.wait("sync", self.t_c, ptin_free[gb])
            ptin_tok[gb] = self.d_ptin.inc(nc.sync.dma_start(
                out=ptin[gb].ap().rearrange("p (j c) -> p j c", c=256)[:, 0:g_cnt, :],
                in_=ptsp[:, g_jt0:g_jt0 + g_cnt, g_ic2 * 256:(g_ic2 + 1) * 256]), 16)
            issued += 1

        for _ in range(3):
            issue_group()
        gcur = 0
        nj = 0
        for ic2 in range(self.NIC2):
            par = ic2 % 2
            njt = min(2 * ic2 + 2, self.NT)
            self.wait("sync", self.v_c, x_free[par])
            x_dma = self.d_x.inc(nc.sync.dma_start(
                out=x_sb[par].ap().rearrange("p (tt e) -> p tt e", e=E),
                in_=x_r[:, 2 * ic2: 2 * ic2 + 2, :]), 16)
            self.wait("tensor", self.v_c, ps_free[par])
            for jt in range(njt):
                g = jt % 4
                if g == 0 and jt > 0 or (jt == 0 and ic2 > 0):
                    pass
                gb = gcur % 3
                pb = nj % NPTS
                self.wait("vector", self.d_ptin, ptin_tok[gb])
                self.wait("vector", self.t_c, pts_free[pb])
                sidx = jt * self.NIC + (ic2 // 2 if self.NIC > 1 else 0)
                vt = self.v_c.inc(nc.vector.tensor_scalar_mul(
                    out=pts[pb][:], in0=ptin[gb][:, g * 256:(g + 1) * 256],
                    scalar1=self.corr[:, sidx:sidx + 1]))
                self.wait("tensor", self.v_c, vt)
                self.wait("tensor", self.d_vin, v_tok[min(jt // 8, nvc - 1)])
                for isl in range(2):
                    for eh in range(2):
                        mm = nc.tensor.matmul(
                            ps[par][isl * 2 + eh][:],
                            pts[pb][:, isl * 128:(isl + 1) * 128],
                            v_sb[:, jt * E + eh * 512: jt * E + (eh + 1) * 512],
                            start=(jt == 0), stop=(jt == njt - 1))
                pts_free[pb] = self.t_c.inc(mm)
                nj += 1
                if g == 3 or jt == njt - 1:
                    ptin_free[gb] = self.t_c.n
                    gcur += 1
                    issue_group()
            ttok = self.t_c.n
            self.wait("vector", self.t_c, ttok)
            self.wait("vector", self.d_x, x_dma)
            self.wait("vector", self.d_accout, acc_free[par])
            last = None
            for isl in range(2):
                for eh in range(2):
                    last = nc.vector.tensor_tensor(
                        out=acc[par][:, isl * E + eh * 512: isl * E + (eh + 1) * 512],
                        in0=ps[par][isl * 2 + eh][:],
                        in1=x_sb[par][:, isl * E + eh * 512: isl * E + (eh + 1) * 512],
                        op=ALU.add)
            ps_free[par] = self.v_c.inc(last)
            x_free[par] = ps_free[par]
            self.wait("sync", self.v_c, ps_free[par])
            acc_free[par] = self.d_accout.inc(nc.sync.dma_start(
                out=self.acc_spill[:, 2 * ic2 * E: (2 * ic2 + 2) * E],
                in_=acc[par][:]), 16)
        self.barrier()
        st.close()
        self.tap("acc", self.acc_spill)

    def emit_moe(self):
        nc, T = self.nc, self.T
        PT, PS = self.PTOK, self.PS
        st = ExitStack()
        sb = lambda nm, shp, dt=F32: st.enter_context(nc.sbuf_tensor(nm, shp, dt))
        y_acc = sb("moe_y", [128, PS * E])
        dbn_bf = sb("moe_dbn", [NEXP, E], BF16)
        probsT_sb = sb("moe_pT", [NEXP, self.NT * 128], BF16)
        hT = sb("moe_h", [128, FT * PT], BF16)
        dw_sb = sb("moe_dw", [128, FT * E], BF16)
        NGW = 3
        gwb = [sb(f"moe_gw{b}", [128, ET * 128], BF16) for b in range(NGW)]
        uwb = [sb(f"moe_uw{b}", [128, ET * 128], BF16) for b in range(NGW)]
        silu = [sb(f"moe_si{b}", [128, 512]) for b in range(4)]
        ps = [st.enter_context(nc.psum_tensor(f"moe_ps{b}", [128, 512], F32))
              for b in range(8)]
        bank_tok = [(None, 0)] * 8
        def bank_wait(i):
            sem, val = bank_tok[i]
            if sem is not None:
                self.wait("tensor", sem, val)
        NTC = PT // 512
        out_r = self.out.rearrange("(tp ts p) e -> tp p ts e", ts=PS, p=128)
        tok_dbb = self.g_moe.inc(nc.gpsimd.dma_start(out=dbn_bf[:],
                                                      in_=self.down_b_n8), 16)
        tok_pT = self.g_moe.inc(nc.gpsimd.dma_start(out=probsT_sb[:],
                                                    in_=self.probsT_hbm), 16)
        self.wait("tensor", self.g_moe, tok_pT)
        self.wait("tensor", self.g_moe, tok_dbb)
        self.wait("gpsimd", self.d_accout, self.d_accout.n)
        self.wait("scalar", self.d_const, self.const_tok)
        gw_tok = [0] * NGW
        uw_tok = [0] * NGW
        wc_free = [0] * NGW
        dwc_free = 0
        dw_tok = 0
        dw_free = 0
        gw_free = [0] * NGW
        silu_free = [0, 0, 0, 0]
        y_in_tok = 0
        y_out_tok = 0
        h_free = 0
        y_in_tok = self.d_accin.inc(nc.gpsimd.dma_start(
            out=y_acc[:], in_=self.acc_spill[:, 0: PS * E]), 16)
        for tp in range(self.NP):
            for e in range(NEXP):
                h_tok = [0] * NTC
                for ft in range(FT):
                    gb = ft % NGW
                    if tp == 0:
                        self.wait("gpsimd", self.t_c, gw_free[gb])
                        self.wait("gpsimd", self.d_wc, wc_free[gb])
                        gw_tok[gb] = self.g_moe.inc(nc.gpsimd.dma_start(
                            out=gwb[gb][:], in_=self.gw[e][:, ft * E:(ft + 1) * E]), 16)
                        uw_tok[gb] = self.g_moe.inc(nc.gpsimd.dma_start(
                            out=uwb[gb][:], in_=self.uw[e][:, ft * E:(ft + 1) * E]), 16)
                        self.wait("sync", self.g_moe, uw_tok[gb])
                        nc.sync.dma_start(out=self.gwc[e][:, ft * E:(ft + 1) * E],
                                          in_=gwb[gb][:]).then_inc(self.d_wc.h, 16)
                        self.d_wc.n += 16
                        wc_free[gb] = self.d_wc.inc(nc.sync.dma_start(
                            out=self.uwc[e][:, ft * E:(ft + 1) * E], in_=uwb[gb][:]), 16)
                        self.wait("tensor", self.g_moe, uw_tok[gb])
                    else:
                        self.wait("sync", self.t_c, gw_free[gb])
                        nc.sync.dma_start(
                            out=gwb[gb][:],
                            in_=self.gwc[e][:, ft * E:(ft + 1) * E]).then_inc(self.d_wl.h, 16)
                        self.d_wl.n += 16
                        uw_tok[gb] = self.d_wl.inc(nc.sync.dma_start(
                            out=uwb[gb][:],
                            in_=self.uwc[e][:, ft * E:(ft + 1) * E]), 16)
                        self.wait("tensor", self.d_wl, uw_tok[gb])
                    for tc in range(NTC):
                        gk = ft * NTC + tc
                        gi = gk % 4
                        psg, psu = ps[gi], ps[4 + gi]
                        if gk % 2 == 0:
                            for w_ in (gk % 4, (gk + 1) % 4):
                                bank_wait(w_)
                                bank_wait(4 + w_)
                        for dt_ in range(ET):
                            nc.tensor.matmul(
                                psg[:], gwb[gb][:, dt_ * 128:(dt_ + 1) * 128],
                                self.xnT[:, dt_ * T + tp * PT + tc * 512:
                                         dt_ * T + tp * PT + (tc + 1) * 512],
                                start=(dt_ == 0), stop=(dt_ == ET - 1))
                        for dt_ in range(ET):
                            mm = nc.tensor.matmul(
                                psu[:], uwb[gb][:, dt_ * 128:(dt_ + 1) * 128],
                                self.xnT[:, dt_ * T + tp * PT + tc * 512:
                                         dt_ * T + tp * PT + (tc + 1) * 512],
                                start=(dt_ == 0), stop=(dt_ == ET - 1))
                        ttok = self.t_c.inc(mm)
                        sp = gi
                        self.wait("scalar", self.t_c, ttok)
                        self.wait("scalar", self.v_c, silu_free[sp])
                        stok = self.s_c.inc(nc.scalar.activation(
                            out=silu[sp][:], in_=psg[:], func=AF.Silu,
                            bias=self.gate_b_sb[:, e * FT + ft: e * FT + ft + 1],
                            scale=1.0))
                        bank_tok[gi] = (self.s_c, stok)
                        self.wait("vector", self.s_c, stok)
                        self.wait("vector", self.t_c, h_free)
                        vv = self.v_c.inc(nc.vector.scalar_tensor_tensor(
                            out=hT[:, ft * PT + tc * 512: ft * PT + (tc + 1) * 512],
                            in0=psu[:],
                            scalar=self.up_b_sb[:, e * FT + ft: e * FT + ft + 1],
                            in1=silu[sp][:], op0=ALU.add, op1=ALU.mult))
                        bank_tok[4 + gi] = (self.v_c, vv)
                        silu_free[sp] = vv
                        h_tok[tc] = vv
                    gw_free[gb] = self.t_c.n
                if e == 0:
                    for ts in range(PS):
                        for dh in range(2):
                            pb = (ts * 2 + dh) % 8
                            if pb % 4 == 0:
                                for w_ in range(4):
                                    bank_wait((pb + w_) % 8)
                            mm = nc.tensor.matmul(
                                ps[pb][:],
                                probsT_sb[:, (tp * PS + ts) * 128:(tp * PS + ts + 1) * 128],
                                dbn_bf[:, dh * 512:(dh + 1) * 512],
                                start=True, stop=True)
                            ttok = self.t_c.inc(mm)
                            self.wait("vector", self.t_c, ttok)
                            self.wait("vector", self.d_accin, y_in_tok)
                            vv = self.v_c.inc(nc.vector.tensor_tensor(
                                out=y_acc[:, ts * E + dh * 512: ts * E + (dh + 1) * 512],
                                in0=ps[pb][:],
                                in1=y_acc[:, ts * E + dh * 512: ts * E + (dh + 1) * 512],
                                op=ALU.add))
                            bank_tok[pb] = (self.v_c, vv)
                if tp == 0:
                    self.wait("gpsimd", self.t_c, dw_free)
                    self.wait("gpsimd", self.d_wc, dwc_free)
                    dw_tok = self.g_moe.inc(nc.gpsimd.dma_start(
                        out=dw_sb[:], in_=self.dw[e]), 16)
                    self.wait("sync", self.g_moe, dw_tok)
                    dwc_free = self.d_wc.inc(nc.sync.dma_start(
                        out=self.dwc[e][:], in_=dw_sb[:]), 16)
                    dw_sem = self.g_moe
                else:
                    self.wait("sync", self.t_c, dw_free)
                    dw_tok = self.d_wl.inc(nc.sync.dma_start(
                        out=dw_sb[:], in_=self.dwc[e][:]), 16)
                    dw_sem = self.d_wl
                self.wait("tensor", dw_sem, dw_tok)
                for ts in range(PS):
                    self.wait("tensor", self.v_c, h_tok[(ts * 128) // 512])
                    k = ts * 2
                    pb0, pb1 = k % 8, (k + 1) % 8
                    if k % 4 == 0:
                        for w_ in range(4):
                            bank_wait((k + w_) % 8)
                    for ft in range(FT):
                        h_ap = hT[:, ft * PT + ts * 128: ft * PT + (ts + 1) * 128]
                        nc.tensor.matmul(ps[pb0][:], h_ap,
                                         dw_sb[:, ft * E: ft * E + 512],
                                         start=(ft == 0), stop=(ft == FT - 1))
                        mm = nc.tensor.matmul(ps[pb1][:], h_ap,
                                              dw_sb[:, ft * E + 512: ft * E + 1024],
                                              start=(ft == 0), stop=(ft == FT - 1))
                    ttok = self.t_c.inc(mm)
                    self.wait("vector", self.t_c, ttok)
                    self.wait("vector", self.d_accin, y_in_tok)
                    pidx = (tp * PS + ts) * NEXP + e
                    for dh, pb in ((0, pb0), (1, pb1)):
                        vv = self.v_c.inc(nc.vector.scalar_tensor_tensor(
                            out=y_acc[:, ts * E + dh * 512: ts * E + (dh + 1) * 512],
                            in0=ps[pb][:],
                            scalar=self.probs[:, pidx:pidx + 1],
                            in1=y_acc[:, ts * E + dh * 512: ts * E + (dh + 1) * 512],
                            op0=ALU.mult, op1=ALU.add))
                        bank_tok[pb] = (self.v_c, vv)
                h_free = self.t_c.n
                dw_free = self.t_c.n
            self.wait("gpsimd", self.v_c, self.v_c.n)
            y_out_tok = self.d_out.inc(nc.gpsimd.dma_start(
                out=out_r[tp], in_=y_acc.ap().rearrange("p (ts e) -> p ts e", e=E)), 16)
            if tp + 1 < self.NP:
                y_in_tok = self.d_accin.inc(nc.gpsimd.dma_start(
                    out=y_acc[:],
                    in_=self.acc_spill[:, (tp + 1) * PS * E:(tp + 2) * PS * E]), 16)
        for s in (self.d_const, self.d_x, self.d_qout, self.d_vout,
                  self.d_ptout, self.d_qin, self.d_vin, self.d_ptin,
                  self.d_accout, self.d_accin, self.d_wc, self.d_wl):
            if s.n:
                nc.sync.wait_ge(s.h, s.n)
        for s in (self.g_w, self.g_moe):
            if s.n:
                nc.gpsimd.wait_ge(s.h, s.n)
        nc.sync.wait_ge(self.d_out.h, self.d_out.n)
        st.close()

    def build(self):
        self.emit_setup()
        with self.nc.sbuf_tensor("kT", [128, ET * self.T], BF16) as kT:
            self.kT = kT
            self.emit_fused01()
            self.emit_A1()
        self.emit_A2()
        self.emit_moe()
        self.stack.close()
        return self.nc


def host_layout(L0_attn_Wq, L0_attn_bq, L0_attn_Wk, L0_attn_bk, L0_attn_Wv,
                L0_attn_bv, L0_ffn_router_w, L0_ffn_gate_w, L0_ffn_gate_b,
                L0_ffn_up_w, L0_ffn_up_b, L0_ffn_down_w, L0_ffn_down_b):
    c = np.ascontiguousarray
    f = lambda a: np.asarray(a, dtype=np.float32)
    d = {}
    d["wqT"] = c(f(L0_attn_Wq).T)
    d["wkT"] = c(f(L0_attn_Wk).T)
    d["wvT"] = c(f(L0_attn_Wv).T)
    d["bq_pp"] = c(f(L0_attn_bq).reshape(ET, 128).T)
    d["bk_pp"] = c(f(L0_attn_bk).reshape(ET, 128).T)
    d["bv_r1"] = c(f(L0_attn_bv).reshape(1, E))
    d["rwT_pp"] = c(f(L0_ffn_router_w).T.reshape(ET, 128, NEXP)
                    .transpose(1, 0, 2).reshape(128, ET * NEXP))
    d["gate_b_pp"] = c(f(L0_ffn_gate_b).reshape(NEXP, FT, 128).transpose(0, 2, 1))
    d["up_b_pp"] = c(f(L0_ffn_up_b).reshape(NEXP, FT, 128).transpose(0, 2, 1))
    d["down_b_n8"] = c(f(L0_ffn_down_b).reshape(NEXP, E))
    d["gw"] = c(f(L0_ffn_gate_w).reshape(NEXP, ET, 128, FT, 128)
                .transpose(0, 2, 3, 1, 4).reshape(NEXP, 128, FT * E))
    d["uw"] = c(f(L0_ffn_up_w).reshape(NEXP, ET, 128, FT, 128)
                .transpose(0, 2, 3, 1, 4).reshape(NEXP, 128, FT * E))
    d["dw"] = c(f(L0_ffn_down_w).reshape(NEXP, FT, 128, E)
                .transpose(0, 2, 1, 3).reshape(NEXP, 128, FT * E))
    return d


def kernel(L0_attn_X, L0_attn_Wq, L0_attn_bq, L0_attn_Wk, L0_attn_bk,
           L0_attn_Wv, L0_attn_bv, L0_ffn_router_w, L0_ffn_gate_w,
           L0_ffn_gate_b, L0_ffn_up_w, L0_ffn_up_b,
           L0_ffn_down_w, L0_ffn_down_b):
    X = np.ascontiguousarray(np.asarray(L0_attn_X, dtype=np.float32))
    B, T, E_ = X.shape
    shared = host_layout(L0_attn_Wq, L0_attn_bq, L0_attn_Wk, L0_attn_bk,
                         L0_attn_Wv, L0_attn_bv, L0_ffn_router_w,
                         L0_ffn_gate_w, L0_ffn_gate_b, L0_ffn_up_w,
                         L0_ffn_up_b, L0_ffn_down_w, L0_ffn_down_b)
    nc = Builder(T=T).build()
    in_maps = [dict(shared, x=np.ascontiguousarray(X[c])) for c in range(B)]
    res = run_bass_kernel_spmd(nc, in_maps, core_ids=list(range(NCORE)))
    out = np.stack([res.results[c]["out"] for c in range(B)], axis=0)
    return out.astype(np.float32)


# version 18
# speedup vs baseline: 1.1119x; 1.1119x over previous
import numpy as np
import ml_dtypes
from contextlib import ExitStack

import concourse.bass as bass
import concourse.mybir as mybir
from concourse.bass_utils import run_bass_kernel_spmd

F32 = mybir.dt.float32
BF16 = mybir.dt.bfloat16
AF = mybir.ActivationFunctionType
ALU = mybir.AluOpType
AX = mybir.AxisListType

E = 1024
FF = 2048
NEXP = 8
ET = E // 128
FT = FF // 128
NCORE = 8

NEG_BIG = -1.0e30


class Sem:
    def __init__(self, stack, nc, name):
        self.h = stack.enter_context(nc.semaphore(name))
        self.n = 0

    def inc(self, instr, k=1):
        instr.then_inc(self.h, k)
        self.n += k
        return self.n


class Builder:
    def __init__(self, T=4096, taps=()):
        self.T = T
        self.NT = T // 128
        self.NIC = T // 512
        self.NIC2 = T // 256
        self.PTOK = min(1024, T)
        self.NP = T // self.PTOK
        self.PS = self.PTOK // 128
        self.taps = set(taps)
        nc = bass.Bass("TRN2", target_bir_lowering=False,
                       detect_race_conditions=False)
        self.nc = nc
        self.stack = ExitStack()
        self._waited = {}

        dt_in = lambda nm, shp: nc.dram_tensor(nm, shp, F32, kind="ExternalInput").ap()
        T_ = self.T
        self.x = dt_in("x", [T_, E])
        self.wqT = dt_in("wqT", [E, E])
        self.wkT = dt_in("wkT", [E, E])
        self.wvT = dt_in("wvT", [E, E])
        self.bq_pp = dt_in("bq_pp", [128, ET])
        self.bk_pp = dt_in("bk_pp", [128, ET])
        self.bv_r1 = dt_in("bv_r1", [1, E])
        self.rwT_pp = dt_in("rwT_pp", [128, ET * NEXP])
        self.gate_b_pp = dt_in("gate_b_pp", [NEXP, 128, FT])
        self.up_b_pp = dt_in("up_b_pp", [NEXP, 128, FT])
        self.down_b_n8 = dt_in("down_b_n8", [NEXP, E])
        self.gw = dt_in("gw", [NEXP, 128, FT * E])
        self.uw = dt_in("uw", [NEXP, 128, FT * E])
        self.dw = dt_in("dw", [NEXP, 128, FT * E])
        self.out = nc.dram_tensor("out", [T_, E], F32, kind="ExternalOutput").ap()
        self.qT_spill = nc.dram_tensor("qT_spill", [128, ET * T_], BF16, kind="Internal").ap()
        self.v_spill = nc.dram_tensor("v_spill", [128, self.NT * E], BF16, kind="Internal").ap()
        self.pt_spill = nc.dram_tensor("pt_spill", [128, self.NT * T_], BF16, kind="Internal").ap()
        self.acc_spill = nc.dram_tensor("acc_spill", [128, self.NT * E], F32, kind="Internal").ap()
        self.probsT_hbm = nc.dram_tensor("probsT_hbm", [NEXP, self.NT * 128], BF16, kind="Internal").ap()
        self.gwc = nc.dram_tensor("gwc", [NEXP, 128, FT * E], BF16, kind="Internal").ap()
        self.uwc = nc.dram_tensor("uwc", [NEXP, 128, FT * E], BF16, kind="Internal").ap()
        self.dwc = nc.dram_tensor("dwc", [NEXP, 128, FT * E], BF16, kind="Internal").ap()

        ident_np = np.eye(128, dtype=ml_dtypes.bfloat16)
        masks_np = np.zeros((128, 4 * 512), dtype=np.float32)
        for m in range(4):
            o = m * 128
            for p in range(128):
                thr = min(o + p, 512)
                masks_np[p, m * 512: m * 512 + thr] = NEG_BIG
        ones_np = np.ones((1, 128), dtype=ml_dtypes.bfloat16)
        consts_np = np.zeros((128, 4), dtype=np.float32)
        consts_np[:, 0] = 1e-5
        consts_np[:, 1] = -0.5
        consts_np[:, 2] = 1.0 / 32.0
        consts_np[:, 3] = -1.0
        self.ident_dram = nc.inline_tensor(ident_np, name="ident_dram")
        self.masks_dram = nc.inline_tensor(masks_np.astype(ml_dtypes.bfloat16), name="masks_dram")
        self.ones_dram = nc.inline_tensor(ones_np, name="ones_dram")
        self.consts_dram = nc.inline_tensor(consts_np, name="consts_dram")

        S = lambda nm: Sem(self.stack, nc, nm)
        self.t_c = S("t_c")
        self.v_c = S("v_c")
        self.s_c = S("s_c")
        self.d_const = S("d_const")
        self.d_x = S("d_x")
        self.d_qout = S("d_qout")
        self.d_vout = S("d_vout")
        self.d_ptout = S("d_ptout")
        self.d_qin = S("d_qin")
        self.d_vin = S("d_vin")
        self.d_ptin = S("d_ptin")
        self.d_accout = S("d_accout")
        self.d_accin = S("d_accin")
        self.d_out = S("d_out")
        self.g_w = S("g_w")
        self.g_moe = S("g_moe")
        self.d_wc = S("d_wc")
        self.d_wl = S("d_wl")

        sb = self.sb = lambda nm, shp, dt=F32: self.stack.enter_context(
            nc.sbuf_tensor(nm, shp, dt))
        self.ident = sb("ident", [128, 128], BF16)
        self.ones_bf = sb("ones_bf", [1, 128], BF16)
        self.consts = sb("consts", [128, 4])
        self.bq_sb = sb("bq_sb", [128, ET])
        self.bks_sb = sb("bks_sb", [128, ET])
        self.bv_bf = sb("bv_bf", [1, E], BF16)
        self.rwT_bf = sb("rwT_bf", [128, ET * NEXP], BF16)
        self.gate_b_sb = sb("gate_b_sb", [128, NEXP * FT])
        self.up_b_sb = sb("up_b_sb", [128, NEXP * FT])
        self.xnT = sb("xnT", [128, ET * T_], BF16)
        self.probs = sb("probs", [128, self.NT * NEXP])
        self.corr = sb("corr", [128, self.NT * self.NIC])
        self.dbg = {}

    def wait(self, eng_name, sem, val):
        if val <= 0:
            return
        key = (eng_name, id(sem))
        if self._waited.get(key, -1) >= val:
            return
        self._waited[key] = val
        getattr(self.nc, eng_name).wait_ge(sem.h, val)

    def vfence(self, instr):
        tok = self.v_c.inc(instr)
        self.nc.vector.wait_ge(self.v_c.h, tok)
        return tok

    def sfence(self, instr):
        tok = self.s_c.inc(instr)
        self.nc.scalar.wait_ge(self.s_c.h, tok)
        return tok

    def barrier(self):
        self.nc.all_engine_barrier()
        self._waited.clear()

    def tap(self, name, ap, dt=F32, shape=None):
        if name not in self.taps:
            return
        shp = shape or list(ap.shape)
        dram = self.nc.dram_tensor("tap_" + name, shp, dt, kind="ExternalOutput").ap()
        s = Sem(self.stack, self.nc, "tapsem_" + name)
        self.nc.sync.dma_start(out=dram, in_=ap).then_inc(s.h, 16)
        self.nc.sync.wait_ge(s.h, 16)
        self.dbg[name] = dram

    def emit_setup(self):
        nc = self.nc
        sy, gp = nc.sync, nc.gpsimd
        dc, gw_ = self.d_const, self.g_w
        dc.inc(sy.dma_start(out=self.ident[:], in_=self.ident_dram.ap()), 16)
        dc.inc(sy.dma_start(out=self.ones_bf[:], in_=self.ones_dram.ap()), 16)
        dc.inc(sy.dma_start(out=self.consts[:], in_=self.consts_dram.ap()), 16)
        dc.inc(sy.dma_start(out=self.bq_sb[:], in_=self.bq_pp), 16)
        dc.inc(sy.dma_start(out=self.bks_sb[:], in_=self.bk_pp), 16)
        dc.inc(sy.dma_start(out=self.gate_b_sb.ap().rearrange("p (e f) -> p e f", f=FT),
                            in_=self.gate_b_pp.rearrange("e p f -> p e f")), 16)
        dc.inc(sy.dma_start(out=self.up_b_sb.ap().rearrange("p (e f) -> p e f", f=FT),
                            in_=self.up_b_pp.rearrange("e p f -> p e f")), 16)
        self.const_tok = dc.n
        self.bqs_sb = self.sb("bqs_sb", [128, ET])
        self.wait("vector", dc, self.const_tok)
        self.bqs_tok = self.v_c.inc(nc.vector.tensor_scalar_mul(
            out=self.bqs_sb[:], in0=self.bq_sb[:], scalar1=1.0 / 32.0))
        self.tok_bv = gw_.inc(gp.dma_start(out=self.bv_bf[:], in_=self.bv_r1), 16)
        self.tok_rw = gw_.inc(gp.dma_start(out=self.rwT_bf[:], in_=self.rwT_pp), 16)

    def emit_fused01(self):
        nc, T = self.nc, self.T
        st = ExitStack()
        sb = lambda nm, shp, dt=F32: st.enter_context(nc.sbuf_tensor(nm, shp, dt))
        x_sb = [sb(f"ln_x{b}", [128, E]) for b in range(4)]
        tmp = sb("ln_tmp", [128, E])
        xnb = [sb(f"ln_xn{b}", [128, E], BF16) for b in range(4)]
        stat2 = sb("ln_stat", [128, 128])
        wv_sb = sb("wv_sb", [128, ET * E], BF16)
        wk_sb = sb("wk_sb", [128, ET * E], BF16)
        wqb = [sb(f"wqb{b}", [128, ET * 128], BF16) for b in range(2)]
        vst = [sb(f"vst{b}", [128, E], BF16) for b in range(2)]
        qst = [sb(f"qst{b}", [128, 512], BF16) for b in range(2)]
        pst = [st.enter_context(nc.psum_tensor(f"ln_ps{b}", [128, 1024], BF16))
               for b in range(4)]
        ps = [st.enter_context(nc.psum_tensor(f"p1_ps{b}", [128, 512], F32))
              for b in range(4)]
        if "lnstat" in self.taps:
            self.lnstat = self.sb("lnstat", [128, self.NT * 8])
        if "lnx" in self.taps:
            self.lnx = self.sb("lnx", [128, self.NT * E])
        wtok = {}
        for nm, dst, srcw in (("v", wv_sb, self.wvT), ("k", wk_sb, self.wkT)):
            wtok[nm] = self.g_w.inc(nc.gpsimd.dma_start(
                out=dst.ap().rearrange("p (et f) -> p et f", f=E),
                in_=srcw.rearrange("(et p) f -> p et f", p=128)), 16)
        wq_r = self.wqT.rearrange("(et p) f -> p et f", p=128)
        x_r = self.x.rearrange("(tt p) e -> tt p e", p=128)
        B0 = 4 if self.NT % 4 == 0 else 1
        NB = self.NT // B0
        x_free = [0] * B0
        ps_tok = [0] * 4
        vst_free = [0, 0]
        self.wait("scalar", self.d_const, self.const_tok)
        self.wait("tensor", self.d_const, self.const_tok)
        self.wait("tensor", self.g_w, wtok["k"])
        ln_rd = [0, 0]
        norm_tok = [[0] * B0 for _ in range(2)]
        xn_guard = [0]

        def emit_stats(bb):
            par = bb % 2
            stat = stat2[:, par * 64: par * 64 + 64]
            x_dma = [0] * B0
            for i in range(B0):
                tt = bb * B0 + i
                if bb >= 1:
                    self.wait("sync", self.v_c, x_free[i])
                x_dma[i] = self.d_x.inc(nc.sync.dma_start(out=x_sb[i][:], in_=x_r[tt]), 16)
            self.wait("vector", self.d_x, x_dma[B0 - 1])
            self.wait("vector", self.s_c, ln_rd[par])
            last = None
            for i in range(B0):
                last = nc.vector.reduce_sum(stat[:, i:i + 1], x_sb[i][:], axis=AX.X)
            self.vfence(last)
            self.vfence(nc.vector.tensor_scalar_mul(out=stat[:, 8:8 + B0],
                                                    in0=stat[:, 0:B0], scalar1=1.0 / E))
            vtok = self.vfence(nc.vector.tensor_mul(stat[:, 16:16 + B0],
                                                    stat[:, 8:8 + B0], stat[:, 8:8 + B0]))
            self.wait("scalar", self.d_x, x_dma[B0 - 1])
            self.wait("scalar", self.v_c, vtok)
            last = None
            for i in range(B0):
                last = nc.scalar.activation(out=tmp[:], in_=x_sb[i][:],
                                            func=AF.Square, accum_out=stat[:, 24 + i:25 + i])
            stok = self.s_c.inc(last)
            self.wait("vector", self.s_c, stok)
            var_tok = self.vfence(nc.vector.scalar_tensor_tensor(
                out=stat[:, 32:32 + B0], in0=stat[:, 24:24 + B0], scalar=1.0 / E,
                in1=stat[:, 16:16 + B0], op0=ALU.mult, op1=ALU.subtract))
            self.wait("scalar", self.v_c, var_tok)
            self.sfence(nc.scalar.activation(out=stat[:, 40:40 + B0], in_=stat[:, 32:32 + B0],
                                             func=AF.Ln, bias=self.consts[:, 0:1], scale=1.0))
            stok = self.s_c.inc(nc.scalar.activation(out=stat[:, 48:48 + B0],
                                                     in_=stat[:, 40:40 + B0],
                                                     func=AF.Exp, scale=self.consts[:, 1:2]))
            ln_rd[par] = stok
            self.wait("vector", self.s_c, stok)
            self.vfence(nc.vector.tensor_scalar_mul(out=stat[:, 56:56 + B0],
                                                    in0=stat[:, 8:8 + B0], scalar1=-1.0))
            self.wait("vector", self.t_c, xn_guard[0])
            for i in range(B0):
                tt = bb * B0 + i
                nc.vector.tensor_single_scalar(out=tmp[:], in_=x_sb[i][:],
                                               scalar=stat[:, 56 + i:57 + i], op=ALU.add)
                norm_tok[par][i] = self.v_c.inc(nc.vector.tensor_single_scalar(
                    out=xnb[i][:], in_=tmp[:], scalar=stat[:, 48 + i:49 + i], op=ALU.mult))
                x_free[i] = norm_tok[par][i]
                if "lnstat" in self.taps:
                    nc.vector.tensor_copy(out=self.lnstat[:, tt * 8 + 0:tt * 8 + 1],
                                          in_=stat[:, 8 + i:9 + i])
                    nc.vector.tensor_copy(out=self.lnstat[:, tt * 8 + 1:tt * 8 + 2],
                                          in_=stat[:, 32 + i:33 + i])

        mmpool = [(None, 0)] * 4
        pool_i = [0]

        def pool_wait(j):
            sem, val = mmpool[j]
            if sem is not None:
                self.wait("tensor", sem, val)

        emit_stats(0)
        for bb in range(NB):
            par = bb % 2
            for i in range(B0):
                tt = bb * B0 + i
                self.wait("tensor", self.v_c, norm_tok[par][i])
                for et in range(ET):
                    pb = et % 4
                    self.wait("tensor", self.v_c if pb % 2 == 0 else self.s_c, ps_tok[pb])
                    tp_tok = self.t_c.inc(nc.tensor.transpose(
                        pst[pb][:, 0:128], xnb[i][:, et * 128:(et + 1) * 128], self.ident[:]))
                    dst = self.xnT[:, et * T + tt * 128: et * T + (tt + 1) * 128]
                    if pb % 2 == 0:
                        self.wait("vector", self.t_c, tp_tok)
                        ps_tok[pb] = self.v_c.inc(nc.vector.tensor_copy(out=dst, in_=pst[pb][:, 0:128]))
                    else:
                        self.wait("scalar", self.t_c, tp_tok)
                        ps_tok[pb] = self.s_c.inc(nc.scalar.copy(out=dst, in_=pst[pb][:, 0:128]))
            xn_guard[0] = self.t_c.n
            xnT_v = self.v_c.n
            xnT_s = self.s_c.n
            if bb + 1 < NB:
                emit_stats(bb + 1)
            self.wait("tensor", self.s_c, xnT_s)
            for i in range(B0):
                tt = bb * B0 + i
                sbuf = vst[tt % 2]
                for fh in range(2):
                    j = pool_i[0] % 4
                    pool_i[0] += 1
                    pool_wait(j)
                    pool_wait(pool_i[0] % 4)
                    for et in range(ET):
                        nc.tensor.matmul(ps[j][:],
                                         self.xnT[:, et * T + tt * 128: et * T + (tt + 1) * 128],
                                         wv_sb[:, et * E + fh * 512: et * E + (fh + 1) * 512],
                                         start=(et == 0), stop=False)
                    ttok = self.t_c.inc(nc.tensor.matmul(
                        ps[j][:], self.ones_bf[:], self.bv_bf[:, fh * 512:(fh + 1) * 512],
                        start=False, stop=True))
                    self.wait("scalar", self.t_c, ttok)
                    if tt >= 2 and fh == 0:
                        self.wait("scalar", self.d_vout, vst_free[tt % 2])
                    mmpool[j] = (self.s_c, self.s_c.inc(nc.scalar.copy(
                        out=sbuf[:, fh * 512:(fh + 1) * 512], in_=ps[j][:])))
                self.wait("sync", self.s_c, self.s_c.n)
                vst_free[tt % 2] = self.d_vout.inc(nc.sync.dma_start(
                    out=self.v_spill[:, tt * E:(tt + 1) * E], in_=sbuf[:]), 16)
            tc = bb
            for ft in range(ET):
                j = pool_i[0] % 4
                pool_i[0] += 1
                pool_wait(j)
                pool_wait(pool_i[0] % 4)
                for et in range(ET):
                    mm = nc.tensor.matmul(
                        ps[j][:],
                        wk_sb[:, et * E + ft * 128: et * E + (ft + 1) * 128],
                        self.xnT[:, et * T + tc * 512: et * T + (tc + 1) * 512],
                        start=(et == 0), stop=(et == ET - 1))
                ttok = self.t_c.inc(mm)
                self.wait("scalar", self.t_c, ttok)
                mmpool[j] = (self.s_c, self.s_c.inc(nc.scalar.activation(
                    out=self.kT[:, ft * T + tc * 512: ft * T + (tc + 1) * 512],
                    in_=ps[j][:], func=AF.Identity,
                    bias=self.bks_sb[:, ft:ft + 1], scale=1.0)))
        self.tap("xnT", self.xnT[:], BF16)
        if "lnstat" in self.taps:
            self.tap("lnstat", self.lnstat[:])
        if "lnx" in self.taps:
            self.tap("lnx", self.lnx[:])
        self.wait("vector", self.d_const, self.const_tok)
        qst_free = [0, 0]
        wq_tok = [0, 0]
        wq_free = [0, 0]
        for ft in range(2):
            wq_tok[ft] = self.g_w.inc(nc.gpsimd.dma_start(
                out=wqb[ft].ap().rearrange("p (et f) -> p et f", f=128),
                in_=wq_r[:, :, ft * 128:(ft + 1) * 128]), 16)
        nq = 0
        for ft in range(ET):
            gb = ft % 2
            self.wait("tensor", self.g_w, wq_tok[gb])
            for tc in range(T // 512):
                pb = pool_i[0] % 4
                pool_i[0] += 1
                pool_wait(pb)
                pool_wait(pool_i[0] % 4)
                for et in range(ET):
                    mm = nc.tensor.matmul(ps[pb][:],
                                          wqb[gb][:, et * 128:(et + 1) * 128],
                                          self.xnT[:, et * T + tc * 512: et * T + (tc + 1) * 512],
                                          start=(et == 0), stop=(et == ET - 1))
                ttok = self.t_c.inc(mm)
                sbuf = qst[nq % 2]
                self.wait("scalar", self.t_c, ttok)
                self.wait("scalar", self.d_qout, qst_free[nq % 2])
                self.wait("scalar", self.v_c, self.bqs_tok)
                mmpool[pb] = (self.s_c, self.s_c.inc(nc.scalar.activation(
                    out=sbuf[:], in_=ps[pb][:], func=AF.Identity,
                    bias=self.bqs_sb[:, ft:ft + 1], scale=self.consts[:, 2:3])))
                self.wait("sync", self.s_c, self.s_c.n)
                qst_free[nq % 2] = self.d_qout.inc(nc.sync.dma_start(
                    out=self.qT_spill[:, ft * T + tc * 512: ft * T + (tc + 1) * 512],
                    in_=sbuf[:]), 16)
                nq += 1
            wq_free[gb] = self.t_c.n
            if ft + 2 < ET:
                self.wait("gpsimd", self.t_c, wq_free[gb])
                wq_tok[gb] = self.g_w.inc(nc.gpsimd.dma_start(
                    out=wqb[gb].ap().rearrange("p (et f) -> p et f", f=128),
                    in_=wq_r[:, :, (ft + 2) * 128:(ft + 3) * 128]), 16)
        self.barrier()
        st.close()

    def emit_A1(self):
        nc, T = self.nc, self.T
        st = ExitStack()
        sb = lambda nm, shp, dt=F32: st.enter_context(nc.sbuf_tensor(nm, shp, dt))
        qc = [sb(f"a1_qc{b}", [128, ET * 512], BF16) for b in range(2)]
        stg = [sb(f"a1_st{b}", [128, 512], BF16) for b in range(2)]
        pts = [sb(f"a1_pt{b}", [128, 512], BF16) for b in range(4)]
        self.masks = sb("masks", [128, 4 * 512], BF16)
        self.tmaxn = sb("tmaxn", [128, self.NT * self.NIC])
        self.sloc = sb("sloc", [128, self.NT * self.NIC])
        mask_tok = self.d_const.inc(nc.sync.dma_start(out=self.masks[:],
                                                      in_=self.masks_dram.ap()), 16)
        NPS = 4
        ps = [st.enter_context(nc.psum_tensor(f"a1_ps{b}", [128, 512], F32))
              for b in range(NPS)]
        u = sb("rt_u", [128, NEXP])
        S_ = sb("rt_S", [128, 1])
        negm = sb("rt_negm", [128, 1])
        ge1 = sb("rt_ge1", [128, NEXP])
        pm = sb("rt_pm", [128, NEXP])
        m2 = sb("rt_m2", [128, 1])
        keep = sb("rt_keep", [128, NEXP])
        pk = sb("rt_pk", [128, NEXP])
        s2 = sb("rt_s2", [128, 1])
        den = sb("rt_den", [128, 1])
        rD = sb("rt_rD", [128, 1])
        probs_bf = sb("rt_pbf", [128, NEXP], BF16)
        rps = [st.enter_context(nc.psum_tensor(f"rt_ps{b}", [128, NEXP], F32))
               for b in range(2)]
        rps_t = st.enter_context(nc.psum_tensor("rt_pst", [NEXP, 128], BF16))
        rps_t2 = st.enter_context(nc.psum_tensor("rt_pst2", [NEXP, 128], BF16))
        rt_state = {"ps_free": [0, 0], "u_free": 0}

        def emit_router_tile(tt):
            pb = tt % 2
            self.wait("tensor", self.s_c, rt_state["ps_free"][pb])
            for et in range(ET):
                mm = nc.tensor.matmul(rps[pb][:],
                                      self.xnT[:, et * T + tt * 128: et * T + (tt + 1) * 128],
                                      self.rwT_bf[:, et * NEXP:(et + 1) * NEXP],
                                      start=(et == 0), stop=(et == ET - 1))
            ttok = self.t_c.inc(mm)
            self.wait("vector", self.t_c, ttok)
            vtok = self.v_c.inc(nc.vector.reduce_max(negm[:], rps[pb][:], axis=AX.X,
                                                     negate=True))
            self.wait("scalar", self.v_c, vtok)
            self.wait("scalar", self.v_c, rt_state["u_free"])
            rt_state["ps_free"][pb] = self.s_c.inc(nc.scalar.activation(
                out=u[:], in_=rps[pb][:], func=AF.Exp, bias=negm[:], scale=1.0,
                accum_out=S_[:]))
            self.wait("vector", self.s_c, rt_state["ps_free"][pb])
            self.vfence(nc.vector.tensor_single_scalar(out=ge1[:], in_=u[:], scalar=1.0,
                                                       op=ALU.is_ge))
            self.vfence(nc.vector.scalar_tensor_tensor(out=pm[:], in0=ge1[:], scalar=NEG_BIG,
                                                       in1=u[:], op0=ALU.mult, op1=ALU.add))
            self.vfence(nc.vector.reduce_max(m2[:], pm[:], axis=AX.X))
            self.vfence(nc.vector.tensor_single_scalar(out=keep[:], in_=u[:], scalar=m2[:],
                                                       op=ALU.is_ge))
            self.vfence(nc.vector.tensor_mul(pk[:], u[:], keep[:]))
            self.vfence(nc.vector.reduce_sum(s2[:], pk[:], axis=AX.X))
            self.vfence(nc.vector.scalar_tensor_tensor(out=den[:], in0=S_[:], scalar=1e-9,
                                                       in1=s2[:], op0=ALU.mult, op1=ALU.add))
            self.vfence(nc.vector.reciprocal(rD[:], den[:]))
            rt_state["u_free"] = self.v_c.inc(nc.vector.tensor_scalar_mul(
                out=self.probs[:, tt * NEXP:(tt + 1) * NEXP], in0=pk[:], scalar1=rD[:]))
        qsp = self.qT_spill.rearrange("p (et t) -> p et t", t=T)
        self.wait("vector", self.d_const, mask_tok)
        self.wait("scalar", self.d_const, self.const_tok)
        nc.vector.memset(self.tmaxn[:], 1.0e30)
        nc.vector.memset(self.sloc[:], 0.0)
        NIC = self.NIC
        ebuf = sb("a1_ebuf", [128, self.NT * NIC])
        zbuf = sb("a1_zbuf", [128, self.NT * NIC])
        minS = sb("a1_minS", [128, self.NT])
        Z = sb("a1_Z", [128, self.NT])
        rZ = sb("a1_rZ", [128, self.NT])
        epi_next = [0]

        def epi_batch(j1):
            j0 = epi_next[0]
            if j0 >= j1:
                return
            epi_next[0] = j1
            self.wait("vector", self.s_c, self.s_c.n)
            last = None
            for jt in range(j0, j1):
                last = nc.vector.tensor_reduce(minS[:, jt:jt + 1],
                                               self.tmaxn[:, jt * NIC:(jt + 1) * NIC],
                                               axis=AX.X, op=ALU.min)
            vtok = self.vfence(last)
            self.wait("scalar", self.v_c, vtok)
            for jt in range(j0, j1):
                last = nc.scalar.activation(out=ebuf[:, jt * NIC:(jt + 1) * NIC],
                                            in_=self.tmaxn[:, jt * NIC:(jt + 1) * NIC],
                                            func=AF.Exp, bias=minS[:, jt:jt + 1],
                                            scale=self.consts[:, 3:4])
            ep_s = self.s_c.inc(last)
            self.wait("vector", self.s_c, ep_s)
            self.vfence(nc.vector.tensor_mul(zbuf[:, j0 * NIC:j1 * NIC],
                                             ebuf[:, j0 * NIC:j1 * NIC],
                                             self.sloc[:, j0 * NIC:j1 * NIC]))
            last = None
            for jt in range(j0, j1):
                last = nc.vector.reduce_sum(Z[:, jt:jt + 1],
                                            zbuf[:, jt * NIC:(jt + 1) * NIC], axis=AX.X)
            self.vfence(last)
            self.vfence(nc.vector.reciprocal(rZ[:, j0:j1], Z[:, j0:j1]))
            last = None
            for jt in range(j0, j1):
                last = nc.vector.tensor_scalar_mul(
                    out=self.corr[:, jt * NIC:(jt + 1) * NIC],
                    in0=ebuf[:, jt * NIC:(jt + 1) * NIC],
                    scalar1=rZ[:, jt:jt + 1])
            self.v_c.inc(last)

        qc_tok = [0, 0]
        qc_free = [0, 0]
        ps_free = [(None, 0)] * NPS
        stg_free = [0, 0]
        pts_free = [0, 0, 0, 0]
        nt = 0
        rt_next = [0]
        self.wait("tensor", self.g_w, self.tok_rw)
        qc_tok[0] = self.d_qin.inc(nc.sync.dma_start(
            out=qc[0].ap().rearrange("p (et c) -> p et c", c=512),
            in_=qsp[:, :, 0:512]), 16)
        for ic in range(self.NIC):
            b = ic % 2
            if ic + 1 < self.NIC:
                nb_ = (ic + 1) % 2
                self.wait("sync", self.t_c, qc_free[nb_])
                qc_tok[nb_] = self.d_qin.inc(nc.sync.dma_start(
                    out=qc[nb_].ap().rearrange("p (et c) -> p et c", c=512),
                    in_=qsp[:, :, (ic + 1) * 512:(ic + 2) * 512]), 16)
            njt = min(4 * ic + 4, self.NT)
            self.wait("tensor", self.d_qin, qc_tok[b])
            for jt in range(njt):
                pb = nt % NPS
                fsem, fval = ps_free[pb]
                if fsem is not None:
                    self.wait("tensor", fsem, fval)
                for ft in range(ET):
                    mm = nc.tensor.matmul(
                        ps[pb][:],
                        self.kT[:, ft * T + jt * 128: ft * T + (jt + 1) * 128],
                        qc[b][:, ft * 512:(ft + 1) * 512],
                        start=(ft == 0), stop=(ft == ET - 1))
                ttok = self.t_c.inc(mm)
                sidx = jt * self.NIC + ic
                m = jt - 4 * ic
                pt = pts[nt % 4]
                if 0 <= m < 4:
                    sg = stg[nt % 2]
                    self.wait("vector", self.t_c, ttok)
                    self.wait("vector", self.s_c, stg_free[nt % 2])
                    sgi = nc.vector.tensor_tensor(
                        out=sg[:], in0=ps[pb][:],
                        in1=self.masks[:, m * 512:(m + 1) * 512], op=ALU.add)
                    ps_free[pb] = (self.v_c, self.vfence(sgi))
                    vtok = self.v_c.inc(nc.vector.reduce_max(
                        self.tmaxn[:, sidx:sidx + 1], sg[:], axis=AX.X, negate=True))
                    self.wait("scalar", self.v_c, vtok)
                    self.wait("scalar", self.d_ptout, pts_free[nt % 4])
                    stg_free[nt % 2] = self.s_c.inc(nc.scalar.activation(
                        out=pt[:], in_=sg[:], func=AF.Exp,
                        bias=self.tmaxn[:, sidx:sidx + 1], scale=1.0,
                        accum_out=self.sloc[:, sidx:sidx + 1]))
                else:
                    self.wait("vector", self.t_c, ttok)
                    vtok = self.v_c.inc(nc.vector.reduce_max(
                        self.tmaxn[:, sidx:sidx + 1], ps[pb][:], axis=AX.X,
                        negate=True))
                    self.wait("scalar", self.v_c, vtok)
                    self.wait("scalar", self.d_ptout, pts_free[nt % 4])
                    ps_free[pb] = (self.s_c, self.s_c.n + 1)
                    self.s_c.inc(nc.scalar.activation(
                        out=pt[:], in_=ps[pb][:], func=AF.Exp,
                        bias=self.tmaxn[:, sidx:sidx + 1], scale=1.0,
                        accum_out=self.sloc[:, sidx:sidx + 1]))
                self.wait("sync", self.s_c, self.s_c.n)
                pts_free[nt % 4] = self.d_ptout.inc(nc.sync.dma_start(
                    out=self.pt_spill[:, jt * T + ic * 512: jt * T + (ic + 1) * 512],
                    in_=pt[:]), 16)
                nt += 1
                if nt % 4 == 2 and rt_next[0] < self.NT:
                    emit_router_tile(rt_next[0])
                    rt_next[0] += 1
                if ic == self.NIC - 1 and (jt + 1) % 8 == 0:
                    epi_batch(jt + 1)
            qc_free[b] = self.t_c.n
        while rt_next[0] < self.NT:
            emit_router_tile(rt_next[0])
            rt_next[0] += 1
        self.tap("probs", self.probs[:])
        epi_batch(self.NT)
        probsT_sb = sb("probsT_sb", [NEXP, self.NT * 128], BF16)
        probs_bf_all = sb("probs_bf_all", [128, self.NT * NEXP], BF16)
        self.wait("vector", self.s_c, self.s_c.n)
        pba = self.v_c.inc(nc.vector.tensor_copy(out=probs_bf_all[:], in_=self.probs[:]))
        self.wait("tensor", self.v_c, pba)
        pstf = [0, 0]
        for tt in range(self.NT):
            self.wait("tensor", self.v_c, pstf[tt % 2])
            tpk = self.t_c.inc(nc.tensor.transpose(
                rps_t[:] if tt % 2 == 0 else rps_t2[:],
                probs_bf_all[:, tt * NEXP:(tt + 1) * NEXP], self.ident[:]))
            self.wait("vector", self.t_c, tpk)
            pstf[tt % 2] = self.v_c.inc(nc.vector.tensor_copy(
                out=probsT_sb[:, tt * 128:(tt + 1) * 128],
                in_=rps_t[:] if tt % 2 == 0 else rps_t2[:]))
        self.wait("sync", self.v_c, self.v_c.n)
        self.d_ptout.inc(nc.sync.dma_start(out=self.probsT_hbm, in_=probsT_sb[:]), 16)
        self.a1_final = (self.t_c.n, self.v_c.n, self.s_c.n)
        st.close()
        self.tap("corr", self.corr[:])

    def emit_A2(self):
        nc, T = self.nc, self.T
        st = ExitStack()
        sb = lambda nm, shp, dt=F32: st.enter_context(nc.sbuf_tensor(nm, shp, dt))
        NPTS = 6
        v_sb = sb("a2_v", [128, self.NT * E], BF16)
        ptin = [sb(f"a2_ptin{b}", [128, 4 * 256], BF16) for b in range(3)]
        pts = [sb(f"a2_pts{b}", [128, 256], BF16) for b in range(NPTS)]
        x_sb = [sb(f"a2_x{b}", [128, 2 * E]) for b in range(2)]
        acc = [sb(f"a2_acc{b}", [128, 2 * E]) for b in range(2)]
        ps = [[st.enter_context(nc.psum_tensor(f"a2_ps{par}_{k}", [128, 512], F32))
               for k in range(4)] for par in range(2)]
        t_f, v_f, s_f = self.a1_final
        self.wait("sync", self.t_c, t_f)
        VC = 8 * E
        nvc = (self.NT * E + VC - 1) // VC
        v_tok = []
        for c in range(nvc):
            lo = c * VC
            hi = min((c + 1) * VC, self.NT * E)
            v_tok.append(self.d_vin.inc(nc.sync.dma_start(
                out=v_sb[:, lo:hi], in_=self.v_spill[:, lo:hi]), 16))
        self.wait("sync", self.v_c, v_f)
        self.wait("sync", self.s_c, s_f)
        self.wait("vector", self.s_c, s_f)
        self.wait("tensor", self.s_c, s_f)
        ptsp = self.pt_spill.rearrange("p (jt t) -> p jt t", t=T)
        x_r = self.x.rearrange("(tt p) e -> p tt e", p=128)
        ptin_tok = [0, 0, 0]
        ptin_free = [0, 0, 0]
        pts_free = [0] * NPTS
        ps_free = [0, 0]
        x_free = [0, 0]
        acc_free = [0, 0]
        groups = []
        for ic2 in range(self.NIC2):
            njt = min(2 * ic2 + 2, self.NT)
            for jt0 in range(0, njt, 4):
                groups.append((ic2, jt0, min(4, njt - jt0)))
        n_groups = len(groups)
        issued = 0

        def issue_group():
            nonlocal issued
            if issued >= n_groups:
                return
            g_ic2, g_jt0, g_cnt = groups[issued]
            gb = issued % 3
            self.wait("sync", self.t_c, ptin_free[gb])
            ptin_tok[gb] = self.d_ptin.inc(nc.sync.dma_start(
                out=ptin[gb].ap().rearrange("p (j c) -> p j c", c=256)[:, 0:g_cnt, :],
                in_=ptsp[:, g_jt0:g_jt0 + g_cnt, g_ic2 * 256:(g_ic2 + 1) * 256]), 16)
            issued += 1

        for _ in range(3):
            issue_group()
        gcur = 0
        nj = 0
        for ic2 in range(self.NIC2):
            par = ic2 % 2
            njt = min(2 * ic2 + 2, self.NT)
            self.wait("sync", self.v_c, x_free[par])
            x_dma = self.d_x.inc(nc.sync.dma_start(
                out=x_sb[par].ap().rearrange("p (tt e) -> p tt e", e=E),
                in_=x_r[:, 2 * ic2: 2 * ic2 + 2, :]), 16)
            self.wait("tensor", self.v_c, ps_free[par])
            for jt in range(njt):
                g = jt % 4
                if g == 0 and jt > 0 or (jt == 0 and ic2 > 0):
                    pass
                gb = gcur % 3
                pb = nj % NPTS
                self.wait("vector", self.d_ptin, ptin_tok[gb])
                self.wait("vector", self.t_c, pts_free[pb])
                sidx = jt * self.NIC + (ic2 // 2 if self.NIC > 1 else 0)
                vt = self.v_c.inc(nc.vector.tensor_scalar_mul(
                    out=pts[pb][:], in0=ptin[gb][:, g * 256:(g + 1) * 256],
                    scalar1=self.corr[:, sidx:sidx + 1]))
                self.wait("tensor", self.v_c, vt)
                self.wait("tensor", self.d_vin, v_tok[min(jt // 8, nvc - 1)])
                for isl in range(2):
                    for eh in range(2):
                        mm = nc.tensor.matmul(
                            ps[par][isl * 2 + eh][:],
                            pts[pb][:, isl * 128:(isl + 1) * 128],
                            v_sb[:, jt * E + eh * 512: jt * E + (eh + 1) * 512],
                            start=(jt == 0), stop=(jt == njt - 1))
                pts_free[pb] = self.t_c.inc(mm)
                nj += 1
                if g == 3 or jt == njt - 1:
                    ptin_free[gb] = self.t_c.n
                    gcur += 1
                    issue_group()
            ttok = self.t_c.n
            self.wait("vector", self.t_c, ttok)
            self.wait("vector", self.d_x, x_dma)
            self.wait("vector", self.d_accout, acc_free[par])
            last = None
            for isl in range(2):
                for eh in range(2):
                    last = nc.vector.tensor_tensor(
                        out=acc[par][:, isl * E + eh * 512: isl * E + (eh + 1) * 512],
                        in0=ps[par][isl * 2 + eh][:],
                        in1=x_sb[par][:, isl * E + eh * 512: isl * E + (eh + 1) * 512],
                        op=ALU.add)
            ps_free[par] = self.v_c.inc(last)
            x_free[par] = ps_free[par]
            self.wait("sync", self.v_c, ps_free[par])
            acc_free[par] = self.d_accout.inc(nc.sync.dma_start(
                out=self.acc_spill[:, 2 * ic2 * E: (2 * ic2 + 2) * E],
                in_=acc[par][:]), 16)
        self.barrier()
        st.close()
        self.tap("acc", self.acc_spill)

    def emit_moe(self):
        nc, T = self.nc, self.T
        PT, PS = self.PTOK, self.PS
        st = ExitStack()
        sb = lambda nm, shp, dt=F32: st.enter_context(nc.sbuf_tensor(nm, shp, dt))
        y_acc = sb("moe_y", [128, PS * E])
        dbn_bf = sb("moe_dbn", [NEXP, E], BF16)
        probsT_sb = sb("moe_pT", [NEXP, self.NT * 128], BF16)
        hT = sb("moe_h", [128, FT * PT], BF16)
        dw_sb = sb("moe_dw", [128, FT * E], BF16)
        NGW = 3
        gwb = [sb(f"moe_gw{b}", [128, ET * 128], BF16) for b in range(NGW)]
        uwb = [sb(f"moe_uw{b}", [128, ET * 128], BF16) for b in range(NGW)]
        silu = [sb(f"moe_si{b}", [128, 512]) for b in range(4)]
        ps = [st.enter_context(nc.psum_tensor(f"moe_ps{b}", [128, 512], F32))
              for b in range(8)]
        bank_tok = [(None, 0)] * 8
        def bank_wait(i):
            sem, val = bank_tok[i]
            if sem is not None:
                self.wait("tensor", sem, val)
        NTC = PT // 512
        out_r = self.out.rearrange("(tp ts p) e -> tp p ts e", ts=PS, p=128)
        tok_dbb = self.g_moe.inc(nc.gpsimd.dma_start(out=dbn_bf[:],
                                                      in_=self.down_b_n8), 16)
        tok_pT = self.g_moe.inc(nc.gpsimd.dma_start(out=probsT_sb[:],
                                                    in_=self.probsT_hbm), 16)
        self.wait("tensor", self.g_moe, tok_pT)
        self.wait("tensor", self.g_moe, tok_dbb)
        self.wait("gpsimd", self.d_accout, self.d_accout.n)
        self.wait("scalar", self.d_const, self.const_tok)
        gw_tok = [0] * NGW
        uw_tok = [0] * NGW
        wc_free = [0] * NGW
        dwc_free = 0
        dw_tok = 0
        dw_free = 0
        gw_free = [0] * NGW
        silu_free = [0, 0, 0, 0]
        y_in_tok = 0
        y_out_tok = 0
        h_free = 0
        y_in_tok = self.d_accin.inc(nc.gpsimd.dma_start(
            out=y_acc[:], in_=self.acc_spill[:, 0: PS * E]), 16)
        for tp in range(self.NP):
            for e in range(NEXP):
                h_tok = [0] * NTC
                for ft in range(FT):
                    gb = ft % NGW
                    if tp == 0:
                        self.wait("gpsimd", self.t_c, gw_free[gb])
                        self.wait("gpsimd", self.d_wc, wc_free[gb])
                        gw_tok[gb] = self.g_moe.inc(nc.gpsimd.dma_start(
                            out=gwb[gb][:], in_=self.gw[e][:, ft * E:(ft + 1) * E]), 16)
                        uw_tok[gb] = self.g_moe.inc(nc.gpsimd.dma_start(
                            out=uwb[gb][:], in_=self.uw[e][:, ft * E:(ft + 1) * E]), 16)
                        self.wait("sync", self.g_moe, uw_tok[gb])
                        nc.sync.dma_start(out=self.gwc[e][:, ft * E:(ft + 1) * E],
                                          in_=gwb[gb][:]).then_inc(self.d_wc.h, 16)
                        self.d_wc.n += 16
                        wc_free[gb] = self.d_wc.inc(nc.sync.dma_start(
                            out=self.uwc[e][:, ft * E:(ft + 1) * E], in_=uwb[gb][:]), 16)
                        self.wait("tensor", self.g_moe, uw_tok[gb])
                    else:
                        self.wait("sync", self.t_c, gw_free[gb])
                        nc.sync.dma_start(
                            out=gwb[gb][:],
                            in_=self.gwc[e][:, ft * E:(ft + 1) * E]).then_inc(self.d_wl.h, 16)
                        self.d_wl.n += 16
                        uw_tok[gb] = self.d_wl.inc(nc.sync.dma_start(
                            out=uwb[gb][:],
                            in_=self.uwc[e][:, ft * E:(ft + 1) * E]), 16)
                        self.wait("tensor", self.d_wl, uw_tok[gb])
                    for tc in range(NTC):
                        gk = ft * NTC + tc
                        gi = gk % 4
                        psg, psu = ps[gi], ps[4 + gi]
                        if gk % 2 == 0:
                            for w_ in (gk % 4, (gk + 1) % 4):
                                bank_wait(w_)
                                bank_wait(4 + w_)
                        for dt_ in range(ET):
                            nc.tensor.matmul(
                                psg[:], gwb[gb][:, dt_ * 128:(dt_ + 1) * 128],
                                self.xnT[:, dt_ * T + tp * PT + tc * 512:
                                         dt_ * T + tp * PT + (tc + 1) * 512],
                                start=(dt_ == 0), stop=(dt_ == ET - 1))
                        for dt_ in range(ET):
                            mm = nc.tensor.matmul(
                                psu[:], uwb[gb][:, dt_ * 128:(dt_ + 1) * 128],
                                self.xnT[:, dt_ * T + tp * PT + tc * 512:
                                         dt_ * T + tp * PT + (tc + 1) * 512],
                                start=(dt_ == 0), stop=(dt_ == ET - 1))
                        ttok = self.t_c.inc(mm)
                        sp = gi
                        self.wait("scalar", self.t_c, ttok)
                        self.wait("scalar", self.v_c, silu_free[sp])
                        stok = self.s_c.inc(nc.scalar.activation(
                            out=silu[sp][:], in_=psg[:], func=AF.Silu,
                            bias=self.gate_b_sb[:, e * FT + ft: e * FT + ft + 1],
                            scale=1.0))
                        bank_tok[gi] = (self.s_c, stok)
                        self.wait("vector", self.s_c, stok)
                        self.wait("vector", self.t_c, h_free)
                        vv = self.v_c.inc(nc.vector.scalar_tensor_tensor(
                            out=hT[:, ft * PT + tc * 512: ft * PT + (tc + 1) * 512],
                            in0=psu[:],
                            scalar=self.up_b_sb[:, e * FT + ft: e * FT + ft + 1],
                            in1=silu[sp][:], op0=ALU.add, op1=ALU.mult))
                        bank_tok[4 + gi] = (self.v_c, vv)
                        silu_free[sp] = vv
                        h_tok[tc] = vv
                    gw_free[gb] = self.t_c.n
                if e == 0:
                    for ts in range(PS):
                        for dh in range(2):
                            pb = (ts * 2 + dh) % 8
                            if pb % 4 == 0:
                                for w_ in range(4):
                                    bank_wait((pb + w_) % 8)
                            mm = nc.tensor.matmul(
                                ps[pb][:],
                                probsT_sb[:, (tp * PS + ts) * 128:(tp * PS + ts + 1) * 128],
                                dbn_bf[:, dh * 512:(dh + 1) * 512],
                                start=True, stop=True)
                            ttok = self.t_c.inc(mm)
                            self.wait("vector", self.t_c, ttok)
                            self.wait("vector", self.d_accin, y_in_tok)
                            vv = self.v_c.inc(nc.vector.tensor_tensor(
                                out=y_acc[:, ts * E + dh * 512: ts * E + (dh + 1) * 512],
                                in0=ps[pb][:],
                                in1=y_acc[:, ts * E + dh * 512: ts * E + (dh + 1) * 512],
                                op=ALU.add))
                            bank_tok[pb] = (self.v_c, vv)
                if tp == 0:
                    self.wait("gpsimd", self.t_c, dw_free)
                    self.wait("gpsimd", self.d_wc, dwc_free)
                    dw_tok = self.g_moe.inc(nc.gpsimd.dma_start(
                        out=dw_sb[:], in_=self.dw[e]), 16)
                    self.wait("sync", self.g_moe, dw_tok)
                    dwc_free = self.d_wc.inc(nc.sync.dma_start(
                        out=self.dwc[e][:], in_=dw_sb[:]), 16)
                    dw_sem = self.g_moe
                else:
                    self.wait("sync", self.t_c, dw_free)
                    dw_tok = self.d_wl.inc(nc.sync.dma_start(
                        out=dw_sb[:], in_=self.dwc[e][:]), 16)
                    dw_sem = self.d_wl
                self.wait("tensor", dw_sem, dw_tok)
                for ts in range(PS):
                    self.wait("tensor", self.v_c, h_tok[(ts * 128) // 512])
                    k = ts * 2
                    pb0, pb1 = k % 8, (k + 1) % 8
                    if k % 4 == 0:
                        for w_ in range(4):
                            bank_wait((k + w_) % 8)
                    for ft in range(FT):
                        h_ap = hT[:, ft * PT + ts * 128: ft * PT + (ts + 1) * 128]
                        nc.tensor.matmul(ps[pb0][:], h_ap,
                                         dw_sb[:, ft * E: ft * E + 512],
                                         start=(ft == 0), stop=(ft == FT - 1))
                        mm = nc.tensor.matmul(ps[pb1][:], h_ap,
                                              dw_sb[:, ft * E + 512: ft * E + 1024],
                                              start=(ft == 0), stop=(ft == FT - 1))
                    ttok = self.t_c.inc(mm)
                    self.wait("vector", self.t_c, ttok)
                    self.wait("vector", self.d_accin, y_in_tok)
                    pidx = (tp * PS + ts) * NEXP + e
                    for dh, pb in ((0, pb0), (1, pb1)):
                        vv = self.v_c.inc(nc.vector.scalar_tensor_tensor(
                            out=y_acc[:, ts * E + dh * 512: ts * E + (dh + 1) * 512],
                            in0=ps[pb][:],
                            scalar=self.probs[:, pidx:pidx + 1],
                            in1=y_acc[:, ts * E + dh * 512: ts * E + (dh + 1) * 512],
                            op0=ALU.mult, op1=ALU.add))
                        bank_tok[pb] = (self.v_c, vv)
                h_free = self.t_c.n
                dw_free = self.t_c.n
            self.wait("gpsimd", self.v_c, self.v_c.n)
            y_out_tok = self.d_out.inc(nc.gpsimd.dma_start(
                out=out_r[tp], in_=y_acc.ap().rearrange("p (ts e) -> p ts e", e=E)), 16)
            if tp + 1 < self.NP:
                y_in_tok = self.d_accin.inc(nc.gpsimd.dma_start(
                    out=y_acc[:],
                    in_=self.acc_spill[:, (tp + 1) * PS * E:(tp + 2) * PS * E]), 16)
        for s in (self.d_const, self.d_x, self.d_qout, self.d_vout,
                  self.d_ptout, self.d_qin, self.d_vin, self.d_ptin,
                  self.d_accout, self.d_accin, self.d_wc, self.d_wl):
            if s.n:
                nc.sync.wait_ge(s.h, s.n)
        for s in (self.g_w, self.g_moe):
            if s.n:
                nc.gpsimd.wait_ge(s.h, s.n)
        nc.sync.wait_ge(self.d_out.h, self.d_out.n)
        st.close()

    def build(self):
        self.emit_setup()
        with self.nc.sbuf_tensor("kT", [128, ET * self.T], BF16) as kT:
            self.kT = kT
            self.emit_fused01()
            self.emit_A1()
        self.emit_A2()
        self.emit_moe()
        self.stack.close()
        return self.nc


def host_layout(L0_attn_Wq, L0_attn_bq, L0_attn_Wk, L0_attn_bk, L0_attn_Wv,
                L0_attn_bv, L0_ffn_router_w, L0_ffn_gate_w, L0_ffn_gate_b,
                L0_ffn_up_w, L0_ffn_up_b, L0_ffn_down_w, L0_ffn_down_b):
    c = np.ascontiguousarray
    f = lambda a: np.asarray(a, dtype=np.float32)
    d = {}
    d["wqT"] = c(f(L0_attn_Wq).T)
    d["wkT"] = c(f(L0_attn_Wk).T)
    d["wvT"] = c(f(L0_attn_Wv).T)
    d["bq_pp"] = c(f(L0_attn_bq).reshape(ET, 128).T)
    d["bk_pp"] = c(f(L0_attn_bk).reshape(ET, 128).T)
    d["bv_r1"] = c(f(L0_attn_bv).reshape(1, E))
    d["rwT_pp"] = c(f(L0_ffn_router_w).T.reshape(ET, 128, NEXP)
                    .transpose(1, 0, 2).reshape(128, ET * NEXP))
    d["gate_b_pp"] = c(f(L0_ffn_gate_b).reshape(NEXP, FT, 128).transpose(0, 2, 1))
    d["up_b_pp"] = c(f(L0_ffn_up_b).reshape(NEXP, FT, 128).transpose(0, 2, 1))
    d["down_b_n8"] = c(f(L0_ffn_down_b).reshape(NEXP, E))
    d["gw"] = c(f(L0_ffn_gate_w).reshape(NEXP, ET, 128, FT, 128)
                .transpose(0, 2, 3, 1, 4).reshape(NEXP, 128, FT * E))
    d["uw"] = c(f(L0_ffn_up_w).reshape(NEXP, ET, 128, FT, 128)
                .transpose(0, 2, 3, 1, 4).reshape(NEXP, 128, FT * E))
    d["dw"] = c(f(L0_ffn_down_w).reshape(NEXP, FT, 128, E)
                .transpose(0, 2, 1, 3).reshape(NEXP, 128, FT * E))
    return d


def kernel(L0_attn_X, L0_attn_Wq, L0_attn_bq, L0_attn_Wk, L0_attn_bk,
           L0_attn_Wv, L0_attn_bv, L0_ffn_router_w, L0_ffn_gate_w,
           L0_ffn_gate_b, L0_ffn_up_w, L0_ffn_up_b,
           L0_ffn_down_w, L0_ffn_down_b):
    X = np.ascontiguousarray(np.asarray(L0_attn_X, dtype=np.float32))
    B, T, E_ = X.shape
    shared = host_layout(L0_attn_Wq, L0_attn_bq, L0_attn_Wk, L0_attn_bk,
                         L0_attn_Wv, L0_attn_bv, L0_ffn_router_w,
                         L0_ffn_gate_w, L0_ffn_gate_b, L0_ffn_up_w,
                         L0_ffn_up_b, L0_ffn_down_w, L0_ffn_down_b)
    nc = Builder(T=T).build()
    in_maps = [dict(shared, x=np.ascontiguousarray(X[c])) for c in range(B)]
    res = run_bass_kernel_spmd(nc, in_maps, core_ids=list(range(NCORE)))
    out = np.stack([res.results[c]["out"] for c in range(B)], axis=0)
    return out.astype(np.float32)
